# version 07
# baseline (speedup 1.0000x reference)
import numpy as np
import ml_dtypes
import concourse.bass as bass
import concourse.mybir as mybir
from concourse.bass_utils import run_bass_kernel_spmd

F32 = mybir.dt.float32
BF16 = mybir.dt.bfloat16
AF = mybir.ActivationFunctionType
ALU = mybir.AluOpType
AX = mybir.AxisListType

D = 1024
S = 2048
NT = 16
H = 16
DH = 64
NPAIR = 8
NBE = 4
LN_EPS = 1e-5
RMS_EPS2 = 6.4e-15
KF_EPS = 1e-8
LN8 = float(np.log(8.0))
CONSTS = [LN_EPS, KF_EPS, 0.5, -0.5, -1.0, LN8]
BF = ml_dtypes.bfloat16


def _np_consts():
    e = np.arange(D)[:, None]
    F = np.zeros((D, D), np.float64)
    F[:, 0:513] = np.cos(2 * np.pi * e * np.arange(513)[None, :] / D)
    F[:, 513:1024] = -np.sin(2 * np.pi * e * np.arange(1, 512)[None, :] / D)
    i_ = np.arange(D)[None, :]
    AB = np.zeros((D, D), np.float64)
    w = np.full((513, 1), 2.0); w[0, 0] = 1.0; w[512, 0] = 1.0
    AB[0:513, :] = w * np.cos(2 * np.pi * np.arange(513)[:, None] * i_ / D) / D
    AB[513:1024, :] = -2.0 * np.sin(2 * np.pi * np.arange(1, 512)[:, None] * i_ / D) / D
    LT = (np.arange(128)[:, None] <= np.arange(128)[None, :]).astype(np.float64)
    ST = np.zeros((16, 16 * 128), np.float64)
    for t in range(16):
        ST[:t, t * 128:(t + 1) * 128] = 1.0
    I128 = np.eye(128)
    ONES = np.ones((128, 512))
    SEL2 = np.zeros((34, 128))
    for m in range(128):
        SEL2[m // 64, m] = 1.0
        SEL2[32 + m // 64, m] = 1.0
    BLK2 = np.zeros((128, 2))
    BLK2[0:64, 0] = 1.0; BLK2[64:128, 1] = 1.0
    cast = lambda a: np.ascontiguousarray(a.astype(BF))
    return dict(F=cast(F), AB=cast(AB), LT=cast(LT), ST=cast(ST), I128=cast(I128),
                ONES=cast(ONES), SEL2=cast(SEL2), BLK2=cast(BLK2))


class Sem:
    def __init__(self, h):
        self.h = h
        self.n = 0
    def bump(self, k):
        self.n += k
        return self.n


class Prog:
    def __init__(self):
        self.nc = bass.Bass("TRN2", target_bir_lowering=False, debug=False)
        self.tape = []

    def em(self, eng, fn, waits=(), inc=None, inc_by=1):
        w = tuple(waits)
        post = inc.bump(inc_by) if inc is not None else None
        def run(e, fn=fn, w=w, inc=inc, inc_by=inc_by):
            for s, v in w:
                e.wait_ge(s.h, v)
            ins = fn(e)
            if inc is not None:
                ins.then_inc(inc.h, inc_by)
        self.tape.append((eng, run))
        return post

    def wait(self, eng, waits):
        w = tuple(waits)
        def run(e, w=w):
            for s, v in w:
                e.wait_ge(s.h, v)
        self.tape.append((eng, run))

    def finalize(self, stack):
        block = stack.enter_context(self.nc.Block())
        tape = self.tape
        def make(engname):
            def body(e):
                for eng, fn in tape:
                    if eng == engname:
                        fn(e)
            return body
        block.sync(make("sync"))
        block.gpsimd(make("gpsimd"))
        block.tensor(make("tensor"))
        block.scalar(make("scalar"))
        block.vector(make("vector"))


def build_program(sublayers=("attn_a", "ffn", "attn_b", "ffn"), n_be=NBE, parts=("wq", "wk", "wv", "wo", "wc", "wfq", "wfk", "wfv")):
    import contextlib
    P = Prog()
    nc = P.nc
    stack = contextlib.ExitStack()
    em, wait = P.em, P.wait

    x_in = nc.dram_tensor("x_in", [n_be, S, D], F32, kind="ExternalInput").ap()
    x_out = nc.dram_tensor("x_out", [n_be, S, D], F32, kind="ExternalOutput").ap()
    dbg = nc.dram_tensor("dbg", [128, 16384], BF16, kind="ExternalOutput").ap()
    dbg2 = nc.dram_tensor("dbg2", [128, 1536], F32, kind="ExternalOutput").ap()
    wi = {}
    for nm, shape in [("w_q", [D, D]), ("w_k", [D, D]), ("w_v", [D, D]), ("w_o", [D, D]),
                      ("ln_g", [D]), ("ln_b", [D]), ("q_g", [DH]), ("k_g", [DH]), ("b_o", [D]),
                      ("bWq", [D, D]), ("bWk", [D, D]), ("bWv", [D, D]),
                      ("Wsh", [D, D]), ("Wex", [D, D]), ("fbias", [D])]:
        wi[nm] = nc.dram_tensor(nm, shape, F32, kind="ExternalInput").ap()

    C = _np_consts()
    ch = {k: nc.inline_tensor(v, k + "_c").ap() for k, v in C.items()}
    cn_np = np.zeros((128, 16), np.float32)
    for i, v in enumerate(CONSTS):
        cn_np[:, i] = v
    cn_h = nc.inline_tensor(np.ascontiguousarray(cn_np), "cn_c").ap()

    scr = {}
    for nm in ["wqT", "wkT", "wvT", "woT", "wcT", "wqF", "wkF", "wvF", "wnat"]:
        scr[nm] = nc.dram_tensor(nm + "_s", [D, D], BF16, kind="Internal").ap()
    pc = lambda a: a.rearrange("(c p) d -> p c d", p=128)
    v3 = lambda a, d=1024: a.rearrange("p (c d) -> p c d", d=d)

    A = lambda n, sh, dt: nc.alloc_sbuf_tensor(n, sh, dt).ap()
    xres = A("xres", [128, NT * D], F32)
    xnT = A("xnT", [128, 8 * S], BF16)
    aB = A("aB", [128, 16384], BF16)
    aC = A("aC", [128, 16384], BF16)
    aW = A("aW", [128, 8192], BF16)
    miscb = A("miscb", [128, 3968], BF16)
    misc32 = A("misc32", [128, 1536], F32)
    totals = A("totals", [16, 1024], BF16)
    sel2 = A("sel2", [34, 128], BF16)
    stbuf = A("stbuf", [16, 2048], BF16)
    bvrow = A("bvrow", [1, 1024], BF16)
    ztb = A("ztb", [128, 1024], BF16)
    trash = A("trash", [128, 1024], BF16)

    Pp = lambda n, sh, dt=F32: nc.alloc_psum_tensor(n, sh, dt).ap()
    P1 = Pp("P1", [128, 2048])
    P2 = Pp("P2", [128, 1024])
    P4 = Pp("P4", [128, 1024])
    PTb = P4.bitcast(BF16)

    mI = miscb[:, 0:128]
    mLT = miscb[:, 128:256]
    mBLK2 = miscb[:, 256:258]
    mONE = miscb[:, 384:896]
    mBO = miscb[:, 896:1920]
    mFB = miscb[:, 1920:2944]
    mZT = miscb[:, 2944:3968]

    sSUM = misc32[:, 0:16]
    sSSQ = misc32[:, 16:32]
    sM1 = misc32[:, 32:48]
    sV1 = misc32[:, 48:64]
    sR1 = misc32[:, 64:80]
    sR12 = misc32[:, 80:96]
    sT16 = misc32[:, 96:112]
    sGQ = misc32[:, 112:113]
    sGK = misc32[:, 113:114]
    sGL = misc32[:, 120:128]
    sCN = misc32[:, 152:168]
    sROW = misc32[:, 512:1536]

    sBETAb = A("sbetab", [128, 8], BF16)

    for i, v in enumerate(CONSTS):
        nc.const_aps.register(F32, v, sCN[:, i:i + 1])

    cQT = aC[:, 0:2048]
    cQN = aC[:, 2048:4096]
    cKT = aC[:, 4096:6144]
    cKN = aC[:, 6144:8192]
    cVA = aC[:, 8192:12288]
    cPT = [[aC[:, 0:1024], aC[:, 1024:2048]],
           [aC[:, 4096:5120], aC[:, 5120:6144]]]
    bcrr = [aC[:, 12288 + k * 1024: 13312 + k * 1024] for k in range(4)]
    wSL = [aW[:, k * 2048:(k + 1) * 2048] for k in range(3)]
    wBIG = aW[:, 0:8192]

    d1 = Sem(stack.enter_context(nc.semaphore("d1")))
    d2 = Sem(stack.enter_context(nc.semaphore("d2")))
    st = Sem(stack.enter_context(nc.semaphore("st")))
    ss = Sem(stack.enter_context(nc.semaphore("ss")))
    sv = Sem(stack.enter_context(nc.semaphore("sv")))

    for dst, src in [(mI, ch["I128"]), (mLT, ch["LT"]), (mONE, ch["ONES"]),
                     (mBLK2, ch["BLK2"]), (sel2, ch["SEL2"]),
                     (stbuf, ch["ST"]), (sCN, cn_h)]:
        em("sync", lambda e, dst=dst, src=src: e.dma_start(out=dst, in_=src), inc=d1, inc_by=16)
    em("sync", lambda e: e.dma_start(out=sGL, in_=wi["ln_g"].rearrange("(c p) -> p c", p=128), allow_slow_non_contiguous=True), inc=d1, inc_by=16)
    em("sync", lambda e: e.dma_start(out=sGQ[0:64, :], in_=wi["q_g"].unsqueeze(1), allow_slow_non_contiguous=True), inc=d1, inc_by=16)
    em("sync", lambda e: e.dma_start(out=sGQ[64:128, :], in_=wi["q_g"].unsqueeze(1), allow_slow_non_contiguous=True), inc=d1, inc_by=16)
    em("sync", lambda e: e.dma_start(out=sGK[0:64, :], in_=wi["k_g"].unsqueeze(1), allow_slow_non_contiguous=True), inc=d1, inc_by=16)
    em("sync", lambda e: e.dma_start(out=sGK[64:128, :], in_=wi["k_g"].unsqueeze(1), allow_slow_non_contiguous=True), inc=d1, inc_by=16)
    em("gpsimd", lambda e: e.dma_start(out=mBO[0:1, :], in_=wi["b_o"].unsqueeze(0)), inc=d2, inc_by=16)
    em("gpsimd", lambda e: e.dma_start(out=mFB[0:1, :], in_=wi["fbias"].unsqueeze(0)), inc=d2, inc_by=16)
    c_beta = em("gpsimd", lambda e: e.dma_start(out=sBETAb, in_=wi["ln_b"].rearrange("(c p) -> p c", p=128), allow_slow_non_contiguous=True),
                inc=d2, inc_by=16)
    consts_d1 = d1.n

    wnatb = aB[:, 0:8192]
    wTb = aC[:, 0:8192]
    fpk = wBIG

    def build_w(name_in, out_T=None, gamma=False, out_F=None, add_in=None, bias_row_dst=None, bias_pos=0):
        c1 = em("gpsimd", lambda e: e.dma_start(out=v3(wnatb), in_=pc(wi[name_in])),
                waits=[(d1, d1.n), (st, st.n), (sv, sv.n)], inc=d2, inc_by=16)
        dep = (d2, c1)
        if add_in is not None:
            c2 = em("gpsimd", lambda e: e.dma_start(out=v3(wTb), in_=pc(wi[add_in])), inc=d2, inc_by=16)
            cv = em("vector", lambda e: e.tensor_tensor(wnatb, wnatb, wTb, ALU.add),
                    waits=[(d2, c2)], inc=sv)
            dep = (sv, cv)
        if out_T is not None:
            c_nat = em("sync", lambda e: e.dma_start(out=pc(scr["wnat"]), in_=v3(wnatb)), waits=[dep], inc=d1, inc_by=16)
            tr = None
            for c in range(8):
                tr = em("sync", lambda e, c=c: e.dma_start(out=wTb[:, c * 1024:(c + 1) * 1024],
                                                           in_=scr["wnat"][:, c * 128:(c + 1) * 128],
                                                           transpose=True),
                        waits=[(d1, c_nat)] if c == 0 else [], inc=d1, inc_by=16)
            if bias_row_dst is not None:
                bp = bias_pos
                for half in range(2):
                    for k in range(8):
                        em("tensor", lambda e, half=half, k=k, bp=bp: e.matmul(
                            P2[bp:bp + 1, half * 512:(half + 1) * 512],
                            sBETAb[:, k:k + 1],
                            wTb[:, k * 1024 + half * 512: k * 1024 + (half + 1) * 512],
                            start=(k == 0), stop=(k == 7),
                            tile_position=(0, bp) if bp else None),
                            waits=[(d1, tr), (d2, c_beta), (ss, ss.n)] if (half == 0 and k == 0) else [],
                            inc=st if (half == 1 and k == 7) else None)
                em("scalar", lambda e, bias_row_dst=bias_row_dst, bp=bp: e.activation(
                    out=bias_row_dst, in_=P2[bp:bp + 1, :], func=AF.Copy), waits=[(st, st.n)], inc=ss)
            if gamma:
                for c in range(8):
                    em("vector", lambda e, c=c: e.tensor_scalar_mul(wTb[:, c * 1024:(c + 1) * 1024],
                                                                    wTb[:, c * 1024:(c + 1) * 1024],
                                                                    sGL[:, c:c + 1]),
                       waits=[(st, st.n), (d1, tr), (d1, consts_d1)] if c == 0 else [], inc=sv)
                dep2 = (sv, sv.n)
            else:
                dep2 = (d1, tr)
            em("sync", lambda e, out_T=out_T: e.dma_start(out=pc(scr[out_T]), in_=v3(wTb)),
               waits=[dep2, (sv, sv.n)], inc=d1, inc_by=16)
        if out_F is not None:
            cf = em("sync", lambda e: e.dma_start(out=v3(fpk), in_=pc(ch["F"])), inc=d1, inc_by=16)
            ev = None
            for dsl in range(8):
                for ec in range(8):
                    for half in range(2):
                        if ec == 0 and half == 0:
                            w_ = [(d1, cf), dep] if dsl == 0 else [(sv, ev)]
                        else:
                            w_ = []
                        em("tensor", lambda e, dsl=dsl, ec=ec, half=half: e.matmul(
                            P2[:, half * 512:(half + 1) * 512],
                            wnatb[:, ec * 1024 + dsl * 128: ec * 1024 + dsl * 128 + 128],
                            fpk[:, ec * 1024 + half * 512: ec * 1024 + half * 512 + 512],
                            start=(ec == 0), stop=(ec == 7)),
                            waits=w_,
                            inc=st if (ec == 7 and half == 1) else None)
                cst = st.n
                ev = em("vector", lambda e, dsl=dsl: e.tensor_copy(wTb[:, dsl * 1024:(dsl + 1) * 1024], P2),
                        waits=[(st, cst)], inc=sv)
            em("sync", lambda e, out_F=out_F: e.dma_start(out=pc(scr[out_F]), in_=v3(wTb)),
               waits=[(sv, ev)], inc=d1, inc_by=16)

    if "wq" in parts: build_w("w_q", out_T="wqT", gamma=True, bias_row_dst=mBO[32:33, :], bias_pos=32)
    if "wk" in parts: build_w("w_k", out_T="wkT", gamma=True, bias_row_dst=mBO[64:65, :], bias_pos=64)
    if "wv" in parts: build_w("w_v", out_T="wvT", gamma=True, bias_row_dst=bvrow)
    if "wo" in parts: build_w("w_o", out_T="woT")
    if "wc" in parts: build_w("Wsh", out_T="wcT", add_in="Wex")
    if "wfq" in parts: build_w("bWq", out_F="wqF")
    if "wfk" in parts: build_w("bWk", out_F="wkF")
    if "wfv" in parts: build_w("bWv", out_F="wvF")

    def ln_xnT(double):
        for t in range(NT):
            xt = xres[:, t * D:(t + 1) * D]
            em("scalar", lambda e, xt=xt, t=t: e.activation(out=trash, in_=xt, func=AF.Square,
                                                            accum_out=sSSQ[:, t:t + 1]),
               waits=[(sv, add_sv[t])], inc=ss)
            em("vector", lambda e, xt=xt, t=t: e.reduce_sum(sSUM[:, t:t + 1], xt, axis=AX.X),
               waits=[(sv, add_sv[t])] if False else [], inc=sv)
        cs_stats, cv_stats = ss.n, sv.n
        c_ = em("vector", lambda e: e.tensor_scalar_mul(sM1, sSUM, 1.0 / D), waits=[(sv, cv_stats)], inc=sv)
        c_ = em("vector", lambda e: e.scalar_tensor_tensor(sT16, sM1, 1.0, sM1, ALU.mult, ALU.mult),
                waits=[(sv, c_)], inc=sv)
        c_ = em("vector", lambda e: e.tensor_scalar_mul(sV1, sSSQ, 1.0 / D),
                waits=[(ss, cs_stats), (sv, c_)], inc=sv)
        cv = em("vector", lambda e: e.tensor_tensor(sV1, sV1, sT16, ALU.subtract), waits=[(sv, c_)], inc=sv)
        em("scalar", lambda e: e.activation(out=sT16, in_=sV1, func=AF.Ln, bias=LN_EPS), waits=[(sv, cv)], inc=ss)
        cr1 = em("scalar", lambda e: e.activation(out=sR1, in_=sT16, func=AF.Exp, scale=-0.5), inc=ss)
        if double:
            c_ = em("vector", lambda e: e.scalar_tensor_tensor(sT16, sR1, 1.0, sR1, ALU.mult, ALU.mult),
                    waits=[(ss, cr1)], inc=sv)
            cv = em("vector", lambda e: e.tensor_tensor(sT16, sV1, sT16, ALU.mult), waits=[(sv, c_)], inc=sv)
            em("scalar", lambda e: e.activation(out=sT16, in_=sT16, func=AF.Ln, bias=LN_EPS),
               waits=[(sv, cv)], inc=ss)
            cr2 = em("scalar", lambda e: e.activation(out=sT16, in_=sT16, func=AF.Exp, scale=-0.5), inc=ss)
            cr12 = em("vector", lambda e: e.tensor_tensor(sR12, sR1, sT16, ALU.mult),
                      waits=[(ss, cr2), (sv, cv)], inc=sv)
            rfin, rsem, rcnt = sR12, sv, cr12
        else:
            rfin, rsem, rcnt = sR1, ss, cr1
        for t in range(NT):
            xt = xres[:, t * D:(t + 1) * D]
            cv_ap = em("vector", lambda e, xt=xt, t=t, rfin=rfin: e.tensor_scalar(
                mZT, xt, sM1[:, t:t + 1], rfin[:, t:t + 1], ALU.subtract, ALU.mult),
                waits=([(rsem, rcnt), (st, st.n), (ss, cs_stats)] if t == 0 else [(st, st.n)]), inc=sv)
            ctr = None
            for cch in range(8):
                ctr = em("tensor", lambda e, cch=cch: e.transpose(PTb[:, cch * 128:(cch + 1) * 128],
                                                                  mZT[:, cch * 128:(cch + 1) * 128], mI),
                         waits=[(sv, cv_ap)] if cch == 0 else [], inc=st if cch == 7 else None)
            em("vector", lambda e, t=t: e.tensor_copy(
                xnT.rearrange("p (c s) -> p c s", c=8)[:, :, t * 128:(t + 1) * 128], PTb[:, 0:1024].rearrange("p (c s) -> p c s", c=8)),
               waits=[(st, ctr)], inc=sv)

    add_sv = [0] * NT
    def residual_add(psum_ap, t, waits):
        xt = xres[:, t * D:(t + 1) * D]
        c = em("vector", lambda e, xt=xt, psum_ap=psum_ap: e.tensor_tensor(xt, xt, psum_ap, ALU.add),
               waits=waits, inc=sv)
        add_sv[t] = c
        return c

    def dense_out(wname, bias_row, src_T, wload_waits):
        cw = em("sync", lambda e, wname=wname: e.dma_start(out=v3(wBIG), in_=pc(scr[wname])),
                waits=wload_waits, inc=d1, inc_by=16)
        sv0, ss0 = sv.n, ss.n
        prot = [P2, P1[:, 0:1024], P1[:, 1024:2048]]
        evs = [None, None, None]
        for t in range(NT):
            ps = prot[t % 3]
            w_ = [(d1, cw)] if t == 0 else []
            if t < 3:
                w_.append((sv, sv0)); w_.append((ss, ss0))
            if evs[t % 3] is not None:
                w_.append((sv, evs[t % 3]))
            for hlf in range(2):
                em("tensor", lambda e, ps=ps, hlf=hlf, bias_row=bias_row: e.matmul(
                    ps[:, hlf * 512:(hlf + 1) * 512], mONE[0:1, 0:128],
                    bias_row[0:1, hlf * 512:(hlf + 1) * 512], start=True, stop=False),
                    waits=w_ if hlf == 0 else [])
            for cch in range(8):
                for hlf in range(2):
                    em("tensor", lambda e, ps=ps, cch=cch, hlf=hlf, t=t, src_T=src_T: e.matmul(
                        ps[:, hlf * 512:(hlf + 1) * 512],
                        src_T[:, cch * S + t * 128: cch * S + t * 128 + 128],
                        wBIG[:, cch * 1024 + hlf * 512: cch * 1024 + (hlf + 1) * 512],
                        start=False, stop=(cch == 7)),
                        inc=st if (cch == 7 and hlf == 1) else None)
            evs[t % 3] = residual_add(ps, t, [(st, st.n)])
        return evs

    def ffn():
        ln_xnT(double=False)
        dense_out("wcT", mFB, xnT, [(st, st.n)])

    def attn_a(dbg_stop=False, dbg_stop2=False):
        ln_xnT(double=True)
        em("vector", lambda e: e.memset(cVA.rearrange("p (t c) -> p t c", c=256)[:, :, 64:192], 1.0),
           waits=[(st, st.n), (ss, ss.n)], inc=sv)

        def stream_slices(g, buf, waits=()):
            cnt = None
            for mi, nm in enumerate(["wqT", "wkT", "wvT"]):
                cnt = em("sync", lambda e, nm=nm, g=g, mi=mi, buf=buf: e.dma_start(
                    out=v3(wSL[mi][:, buf * 1024:(buf + 1) * 1024], d=128),
                    in_=pc(scr[nm][:, g * 128:(g + 1) * 128])),
                    waits=waits if mi == 0 else (), inc=d1, inc_by=16)
            return cnt

        slice_cw = [None, None]
        slice_cw[0] = stream_slices(0, 0)
        pair_done_sv = None
        for g in range(NPAIR):
            buf = g % 2
            if g + 1 < NPAIR:
                slice_cw[(g + 1) % 2] = stream_slices(g + 1, (g + 1) % 2, waits=[(st, st.n)])
            sl = lambda mi, buf=buf: wSL[mi][:, buf * 1024:(buf + 1) * 1024]

            pair_sv, pair_ss = sv.n, ss.n

            def proj_mm(mi, bp, w0, sqbuf):
                for q4 in range(4):
                    em("tensor", lambda e, q4=q4, bp=bp, g=g: e.matmul(
                        P1[:, q4 * 512:(q4 + 1) * 512],
                        mBO[bp:bp + 1, g * 128:(g + 1) * 128],
                        mONE[bp:bp + 1, 0:512], start=True, stop=False),
                        waits=w0 if q4 == 0 else [])
                slap = sl(mi)
                for cch in range(8):
                    for q4 in range(4):
                        em("tensor", lambda e, cch=cch, q4=q4, slap=slap: e.matmul(
                            P1[:, q4 * 512:(q4 + 1) * 512],
                            slap[:, cch * 128:(cch + 1) * 128],
                            xnT[:, cch * S + q4 * 512: cch * S + (q4 + 1) * 512],
                            start=False, stop=(cch == 7)),
                            inc=st if (cch == 7 and q4 == 3) else None)
                cm = st.n
                return em("scalar", lambda e, sqbuf=sqbuf: e.activation(out=sqbuf, in_=P1, func=AF.Square),
                          waits=[(st, cm)], inc=ss)

            def chain_mm(c1, gcol, sqbuf, nrm, kside, p2_sv, p2_ss):
                cqn = None
                last_cev = None
                for h in range(2):
                    wssq = [(ss, c1), (sv, p2_sv)] + ([(ss, p2_ss)] if last_cev is None else [(ss, last_cev)])
                    em("tensor", lambda e, h=h, sqbuf=sqbuf: e.matmul(
                        P2[0:2, 0:512], mBLK2, sqbuf[:, h * 1024: h * 1024 + 512],
                        start=True, stop=True), waits=wssq)
                    em("tensor", lambda e, h=h, sqbuf=sqbuf: e.matmul(
                        P2[0:2, 512:1024], mBLK2, sqbuf[:, h * 1024 + 512: (h + 1) * 1024],
                        start=True, stop=True), inc=st)
                    cssq = st.n
                    cvm = em("vector", lambda e: e.tensor_scalar_max(P2[0:2, :], P2[0:2, :], RMS_EPS2),
                             waits=[(st, cssq)], inc=sv)
                    cl = em("scalar", lambda e: e.activation(out=P2[0:2, :], in_=P2[0:2, :], func=AF.Ln),
                            waits=[(sv, cvm)], inc=ss)
                    if kside:
                        cex = em("scalar", lambda e: e.activation(out=sROW[0:2, :], in_=P2[0:2, :],
                                                                  func=AF.Exp, scale=-0.5, bias=LN8),
                                 waits=[(ss, cl)], inc=ss)
                    else:
                        cex = em("scalar", lambda e: e.activation(out=sROW[0:2, :], in_=P2[0:2, :],
                                                                  func=AF.Exp, scale=-0.5),
                                 waits=[(ss, cl)], inc=ss)
                    chi = em("scalar", lambda e: e.activation(out=cPT[0][0][0:2, :], in_=sROW[0:2, :],
                                                              func=AF.Copy), waits=[(ss, cex)], inc=ss)
                    clo = em("vector", lambda e: e.tensor_tensor(cPT[0][1][0:2, :], sROW[0:2, :],
                                                                 cPT[0][0][0:2, :], ALU.subtract),
                             waits=[(ss, chi)], inc=sv)
                    for q2 in range(2):
                        em("tensor", lambda e, q2=q2: e.matmul(
                            P2[:, q2 * 512:(q2 + 1) * 512], sel2[0:2, :],
                            cPT[0][0][0:2, q2 * 512:(q2 + 1) * 512], start=True, stop=False),
                            waits=[(sv, clo)] if q2 == 0 else [])
                        em("tensor", lambda e, q2=q2: e.matmul(
                            P2[:, q2 * 512:(q2 + 1) * 512], sel2[0:2, :],
                            cPT[0][1][0:2, q2 * 512:(q2 + 1) * 512], start=False, stop=True),
                            inc=st if q2 == 1 else None)
                    cbc = st.n
                    cev = em("scalar", lambda e: e.activation(out=sROW, in_=P2, func=AF.Copy),
                             waits=[(st, cbc)], inc=ss)
                    last_cev = cev
                    cqn = em("vector", lambda e, h=h, nrm=nrm, gcol=gcol: e.scalar_tensor_tensor(
                        nrm[:, h * 1024:(h + 1) * 1024], P1[:, h * 1024:(h + 1) * 1024], gcol,
                        sROW, ALU.mult, ALU.mult), waits=[(ss, cev)], inc=sv)
                return cqn

            def vproj_range(t0_, t1_, vstate):
                for t in range(t0_, t1_):
                    slot = t % 8
                    w_ = []
                    if t == 0:
                        w_ = [(sv, pair_sv), (ss, pair_ss)]
                    if t >= 8 and t % 8 == 0:
                        w_ = [(sv, vstate["cvev"])]
                    em("tensor", lambda e, slot=slot, g=g: e.matmul(
                        P4[:, slot * 128:(slot + 1) * 128], mONE[0:1, 0:128],
                        bvrow[0:1, g * 128:(g + 1) * 128], start=True, stop=False), waits=w_)
                    slap2 = sl(2)
                    for cch in range(8):
                        em("tensor", lambda e, t=t, cch=cch, slot=slot, slap2=slap2: e.matmul(
                            P4[:, slot * 128:(slot + 1) * 128],
                            xnT[:, cch * S + t * 128: cch * S + t * 128 + 128],
                            slap2[:, cch * 128:(cch + 1) * 128],
                            start=False, stop=(cch == 7)),
                            inc=st if cch == 7 else None)
                    if t % 8 == 7:
                        cpv = st.n
                        tb = t - 7
                        vA = cVA.rearrange("p (t x) -> p t x", x=256)[:, tb:tb + 8, 0:64]
                        vB = cVA.rearrange("p (t x) -> p t x", x=256)[:, tb:tb + 8, 192:256]
                        pA = P4.rearrange("p (t u c) -> p t u c", t=8, u=2, c=64)[:, :, 0, :]
                        pB = P4.rearrange("p (t u c) -> p t u c", t=8, u=2, c=64)[:, :, 1, :]
                        em("vector", lambda e, vA=vA, pA=pA: e.tensor_copy(vA, pA),
                           waits=[(st, cpv)], inc=sv)
                        vstate["cvev"] = em("vector", lambda e, vB=vB, pB=pB: e.tensor_copy(vB, pB), inc=sv)

            vstate = {"cvev": None}
            w0 = [(d1, slice_cw[buf]), (sv, pair_sv), (ss, pair_ss)]
            c1q = proj_mm(0, 32, w0, cQT)
            vproj_range(0, 16, vstate)
            cq = chain_mm(c1q, sGQ, cQT, cQN, False, pair_sv, pair_ss)
            c1k = proj_mm(1, 64, [(sv, cq)], cKT)
            ck = chain_mm(c1k, sGK, cKT, cKN, True, pair_sv, pair_ss)
            crms = ck
            cvev = vstate["cvev"]
            if dbg_stop:
                wait("sync", [(sv, sv.n), (ss, ss.n), (st, st.n)])
                em("sync", lambda e: e.dma_start(out=dbg[:, 0:2048], in_=cQN), inc=d1, inc_by=16)
                em("sync", lambda e: e.dma_start(out=dbg[:, 2048:4096], in_=cKN), inc=d1, inc_by=16)
                em("sync", lambda e: e.dma_start(out=dbg[:, 4096:6176], in_=cVA), inc=d1, inc_by=16)
                return
            PSIM = [P1[:, 0:1024], P1[:, 1024:2048]]
            PPV = [P2, P4]
            simev = [None, None]
            pvdone = [[None, None], [None, None]]
            otv_prev = [cvev, cvev]
            for ih in range(2):
                for jt in range(NT):
                    csims = [None, None]
                    for hh in range(2):
                        pb = 64 * hh
                        w_ = []
                        if simev[hh] is not None:
                            w_.append((ss, simev[hh]))
                        elif jt == 0:
                            w_.append((sv, crms))
                        for q2 in range(2):
                            em("tensor", lambda e, hh=hh, q2=q2, pb=pb, jt=jt, ih=ih: e.matmul(
                                PSIM[hh][:, q2 * 512:(q2 + 1) * 512],
                                cKN[pb:pb + 64, jt * 128:(jt + 1) * 128],
                                cQN[pb:pb + 64, ih * 1024 + q2 * 512: ih * 1024 + (q2 + 1) * 512],
                                start=True, stop=True,
                                tile_position=(pb, 0)),
                                waits=w_ if q2 == 0 else [],
                                inc=st if q2 == 1 else None)
                        csims[hh] = st.n
                    for hh in range(2):
                        wexp = [(st, csims[hh])]
                        if pvdone[hh][jt % 2] is not None:
                            wexp.append((st, pvdone[hh][jt % 2]))
                        simev[hh] = em("scalar", lambda e, hh=hh, jt=jt: e.activation(
                            out=cPT[hh][jt % 2], in_=PSIM[hh], func=AF.Exp), waits=wexp, inc=ss)
                    for hh in range(2):
                        aug = cVA[:, jt * 256 + hh * 128: jt * 256 + (hh + 1) * 128]
                        wpv = [(ss, simev[hh])]
                        if jt == 0:
                            wpv.append((sv, otv_prev[hh]))
                        for q2 in range(2):
                            em("tensor", lambda e, hh=hh, q2=q2, jt=jt, aug=aug: e.matmul(
                                PPV[hh][:, q2 * 512:(q2 + 1) * 512], aug,
                                cPT[hh][jt % 2][:, q2 * 512:(q2 + 1) * 512],
                                start=(jt == 0), stop=(jt == NT - 1)),
                                waits=wpv if q2 == 0 else [],
                                inc=st if q2 == 1 else None)
                        pvdone[hh][jt % 2] = st.n
                for hh in range(2):
                    i = hh
                    pv15 = pvdone[hh][(NT - 1) % 2]
                    reg = bcrr[2 * hh + ih]
                    if hh == 0:
                        rsrows, otrows = slice(64, 128), slice(0, 64)
                        dst = aB[0:64, g * S + ih * 1024: g * S + (ih + 1) * 1024]
                    else:
                        rsrows, otrows = slice(0, 64), slice(64, 128)
                        dst = aB[64:128, g * S + ih * 1024: g * S + (ih + 1) * 1024]
                    cl = em("scalar", lambda e, i=i, rsrows=rsrows: e.activation(
                        out=PPV[i][rsrows, :], in_=PPV[i][rsrows, :], func=AF.Ln),
                        waits=[(st, pv15)], inc=ss)
                    ce = em("scalar", lambda e, i=i, rsrows=rsrows, reg=reg: e.activation(
                        out=reg[rsrows, :], in_=PPV[i][rsrows, :], func=AF.Exp, scale=-1.0),
                        waits=[(ss, cl)], inc=ss)
                    cdm = em("sync", lambda e, reg=reg, rsrows=rsrows, otrows=otrows: e.dma_start(
                        out=reg[otrows, :], in_=reg[rsrows, :]), waits=[(ss, ce)], inc=d1, inc_by=16)
                    otv_prev[hh] = em("vector", lambda e, i=i, otrows=otrows, reg=reg, dst=dst: e.tensor_tensor(
                        dst, PPV[i][otrows, :], reg[otrows, :], ALU.mult),
                        waits=[(d1, cdm)], inc=sv)
            pair_done_sv = sv.n
            if dbg_stop2 and g == NPAIR - 1:
                wait("sync", [(sv, sv.n), (ss, ss.n), (st, st.n), (d1, d1.n)])
                em("sync", lambda e: e.dma_start(out=dbg, in_=aB), inc=d1, inc_by=16)
                return
        dense_out("woT", mBO, aB, [(st, st.n), (sv, sv.n)])

    def attn_b():
        ln_xnT(double=False)

        slotfree = {0: None, 1: None, 2: None}

        def fft_proj(nm, out_arena, extra0):
            cw = em("sync", lambda e, nm=nm: e.dma_start(out=v3(wBIG), in_=pc(scr[nm])),
                    waits=[(st, st.n)], inc=d1, inc_by=16)
            prot = [P2, P1[:, 0:1024], P1[:, 1024:2048]]
            evs = [None, None, None]
            last = None
            for t in range(NT):
                ps = prot[t % 3]
                w_ = ([(d1, cw)] + extra0) if t == 0 else []
                if t < 3 and slotfree[t % 3] is not None:
                    w_.append(slotfree[t % 3])
                if evs[t % 3] is not None:
                    w_.append((sv, evs[t % 3]))
                for cch in range(8):
                    for hlf in range(2):
                        em("tensor", lambda e, ps=ps, cch=cch, hlf=hlf, t=t: e.matmul(
                            ps[:, hlf * 512:(hlf + 1) * 512],
                            xnT[:, cch * S + t * 128: cch * S + t * 128 + 128],
                            wBIG[:, cch * 1024 + hlf * 512: cch * 1024 + (hlf + 1) * 512],
                            start=(cch == 0), stop=(cch == 7)),
                            waits=w_ if (cch == 0 and hlf == 0) else [],
                            inc=st if (cch == 7 and hlf == 1) else None)
                evs[t % 3] = em("vector", lambda e, ps=ps, t=t, out_arena=out_arena: e.tensor_copy(
                    out_arena[:, t * 1024:(t + 1) * 1024], ps), waits=[(st, st.n)], inc=sv)
                slotfree[t % 3] = (sv, evs[t % 3])
                last = evs[t % 3]
            return last

        cfv = fft_proj("wvF", aC, [])
        cwk = em("sync", lambda e: e.dma_start(out=v3(wBIG), in_=pc(scr["wkF"])), waits=[(st, st.n)], inc=d1, inc_by=16)
        prot = [P2, P1[:, 0:1024], P1[:, 1024:2048]]
        evps = [None, None, None]
        fkv_last = None
        for t in range(NT):
            ps = prot[t % 3]
            w_ = [(d1, cwk)] if t == 0 else []
            if t < 3 and slotfree[t % 3] is not None:
                w_.append(slotfree[t % 3])
            if t == 0:
                w_.append((sv, cfv))
            if evps[t % 3] is not None:
                w_.append((ss, evps[t % 3]))
            for cch in range(8):
                for hlf in range(2):
                    em("tensor", lambda e, ps=ps, cch=cch, hlf=hlf, t=t: e.matmul(
                        ps[:, hlf * 512:(hlf + 1) * 512],
                        xnT[:, cch * S + t * 128: cch * S + t * 128 + 128],
                        wBIG[:, cch * 1024 + hlf * 512: cch * 1024 + (hlf + 1) * 512],
                        start=(cch == 0), stop=(cch == 7)),
                        waits=w_ if (cch == 0 and hlf == 0) else [],
                        inc=st if (cch == 7 and hlf == 1) else None)
            cfk = st.n
            fkt = mZT
            c0 = em("scalar", lambda e, ps=ps: e.activation(out=fkt, in_=ps, func=AF.Copy),
                    waits=[(st, cfk), (sv, sv.n)], inc=ss)
            evps[t % 3] = c0
            slotfree[t % 3] = (ss, c0)
            tile = aC[:, t * 1024:(t + 1) * 1024]
            vre, vim = tile[:, 0:513], tile[:, 513:1024]
            kre, kim = fkt[:, 0:513], fkt[:, 513:1024]
            ss_ = sROW[:, 0:513]
            t2 = sROW[:, 513:1024]
            pbuf = aB[:, 15 * 1024:16 * 1024]
            ppre, ppim = pbuf[:, 0:513], pbuf[:, 513:1024]
            em("vector", lambda e, ss_=ss_, kre=kre: e.tensor_tensor(ss_, kre, kre, ALU.mult),
               waits=[(ss, c0)], inc=sv)
            em("vector", lambda e, t2=t2, kim=kim: e.tensor_tensor(t2, kim, kim, ALU.mult), inc=sv)
            c3 = em("vector", lambda e, ss_=ss_, t2=t2: e.tensor_tensor(ss_[:, 1:512], ss_[:, 1:512], t2, ALU.add), inc=sv)
            em("scalar", lambda e, ss_=ss_: e.activation(out=ss_, in_=ss_, func=AF.Ln), waits=[(sv, c3)], inc=ss)
            em("scalar", lambda e, ss_=ss_: e.activation(out=ss_, in_=ss_, func=AF.Exp, scale=0.5), inc=ss)
            em("scalar", lambda e, ss_=ss_: e.activation(out=ss_, in_=ss_, func=AF.Ln, bias=KF_EPS), inc=ss)
            c7 = em("scalar", lambda e, ss_=ss_: e.activation(out=ss_, in_=ss_, func=AF.Exp, scale=-1.0), inc=ss)
            em("vector", lambda e, ppre=ppre, kre=kre, vre=vre: e.tensor_tensor(ppre, kre, vre, ALU.mult), inc=sv)
            em("vector", lambda e, t2=t2, kim=kim, vim=vim: e.tensor_tensor(t2[:, 0:511], kim, vim, ALU.mult), inc=sv)
            em("vector", lambda e, ppre=ppre, t2=t2: e.tensor_tensor(ppre[:, 1:512], ppre[:, 1:512], t2[:, 0:511], ALU.subtract), inc=sv)
            em("vector", lambda e, ppim=ppim, kre=kre, vim=vim: e.tensor_tensor(ppim, kre[:, 1:512], vim, ALU.mult), inc=sv)
            em("vector", lambda e, t2=t2, kim=kim, vre=vre: e.tensor_tensor(t2[:, 0:511], kim, vre[:, 1:512], ALU.mult), inc=sv)
            em("vector", lambda e, ppim=ppim, t2=t2: e.tensor_tensor(ppim, ppim, t2[:, 0:511], ALU.add), inc=sv)
            em("vector", lambda e, vre=vre, ppre=ppre, ss_=ss_: e.tensor_tensor(vre, ppre, ss_, ALU.mult),
               waits=[(ss, c7)], inc=sv)
            fkv_last = em("vector", lambda e, vim=vim, ppim=ppim, ss_=ss_: e.tensor_tensor(
                vim, ppim, ss_[:, 1:512], ALU.mult), inc=sv)
        cfq = fft_proj("wqF", aB, [(sv, fkv_last)])
        cfqn = None
        for t in range(NT):
            tile = aB[:, t * 1024:(t + 1) * 1024]
            re, im = tile[:, 0:513], tile[:, 513:1024]
            ss_ = sROW[:, 0:513]
            t2 = sROW[:, 513:1024]
            em("vector", lambda e, ss_=ss_, re=re: e.tensor_tensor(ss_, re, re, ALU.mult),
               waits=[(sv, cfq)] if t == 0 else [], inc=sv)
            em("vector", lambda e, t2=t2, im=im: e.tensor_tensor(t2[:, 0:511], im, im, ALU.mult), inc=sv)
            c3 = em("vector", lambda e, ss_=ss_, t2=t2: e.tensor_tensor(ss_[:, 1:512], ss_[:, 1:512], t2[:, 0:511], ALU.add), inc=sv)
            em("scalar", lambda e, ss_=ss_: e.activation(out=ss_, in_=ss_, func=AF.Ln), waits=[(sv, c3)], inc=ss)
            em("scalar", lambda e, ss_=ss_: e.activation(out=ss_, in_=ss_, func=AF.Exp, scale=0.5), inc=ss)
            em("scalar", lambda e, ss_=ss_: e.activation(out=ss_, in_=ss_, func=AF.Ln, bias=KF_EPS), inc=ss)
            c7 = em("scalar", lambda e, ss_=ss_: e.activation(out=ss_, in_=ss_, func=AF.Exp, scale=-1.0), inc=ss)
            em("vector", lambda e, re=re, ss_=ss_: e.tensor_tensor(re, re, ss_, ALU.mult), waits=[(ss, c7)], inc=sv)
            cfqn = em("vector", lambda e, im=im, ss_=ss_: e.tensor_tensor(im, im, ss_[:, 1:512], ALU.mult), inc=sv)
        ctot = None
        ev_prev = None
        for t in range(NT):
            w_ = [(sv, cfqn)] if t == 0 else []
            if t == 0 and slotfree[0] is not None:
                w_.append(slotfree[0])
            if ev_prev is not None:
                w_.append((ss, ev_prev))
            for hlf in range(2):
                em("tensor", lambda e, t=t, hlf=hlf: e.matmul(
                    P2[:, hlf * 512:(hlf + 1) * 512], mLT,
                    aC[:, t * 1024 + hlf * 512: t * 1024 + (hlf + 1) * 512],
                    start=True, stop=True),
                    waits=w_ if hlf == 0 else [], inc=st if hlf == 1 else None)
            ccum = st.n
            ev_prev = em("scalar", lambda e, t=t: e.activation(out=aC[:, t * 1024:(t + 1) * 1024], in_=P2,
                                                               func=AF.Copy), waits=[(st, ccum)], inc=ss)
            ctot = em("sync", lambda e, t=t: e.dma_start(out=totals[t:t + 1, :],
                                                         in_=aC[127:128, t * 1024:(t + 1) * 1024]),
                      waits=[(ss, ev_prev)], inc=d1, inc_by=16)
        cab = em("sync", lambda e: e.dma_start(out=v3(wBIG), in_=pc(ch["AB"])), waits=[(st, st.n)], inc=d1, inc_by=16)
        prot2 = [P2, P1[:, 0:1024]]
        evs2 = [None, None]
        ctr_prev = None
        ztc_prev = None
        for t in range(NT):
            tile = aC[:, t * 1024:(t + 1) * 1024]
            ps = prot2[t % 2]
            w_ = [(d1, ctot)] if t == 0 else []
            if evs2[t % 2] is not None:
                w_.append((sv, evs2[t % 2]))
            if t > 0:
                for hlf in range(2):
                    em("tensor", lambda e, t=t, hlf=hlf, ps=ps: e.matmul(
                        ps[:, hlf * 512:(hlf + 1) * 512], stbuf[:, t * 128:(t + 1) * 128],
                        totals[:, hlf * 512:(hlf + 1) * 512], start=True, stop=True),
                        waits=w_ if hlf == 0 else [], inc=st if hlf == 1 else None)
                coff = st.n
                czadd = em("vector", lambda e, tile=tile, ps=ps: e.tensor_tensor(tile, tile, ps, ALU.add),
                           waits=[(st, coff)], inc=sv)
            else:
                wait("vector", w_)
                czadd = sv.n
            zbuf = mZT
            fq = aB[:, t * 1024:(t + 1) * 1024]
            cre, cim = tile[:, 0:513], tile[:, 513:1024]
            qre, qim = fq[:, 0:513], fq[:, 513:1024]
            zre, zim = zbuf[:, 0:513], zbuf[:, 513:1024]
            t2 = sROW[:, 0:511]
            em("vector", lambda e, zre=zre, cre=cre, qre=qre: e.tensor_tensor(zre, cre, qre, ALU.mult),
               waits=[(sv, czadd)] + ([(st, ctr_prev)] if ctr_prev is not None else []), inc=sv)
            em("vector", lambda e, t2=t2, cim=cim, qim=qim: e.tensor_tensor(t2, cim, qim, ALU.mult), inc=sv)
            em("vector", lambda e, zre=zre, t2=t2: e.tensor_tensor(zre[:, 1:512], zre[:, 1:512], t2, ALU.add), inc=sv)
            em("vector", lambda e, zim=zim, cim=cim, qre=qre: e.tensor_tensor(zim, cim, qre[:, 1:512], ALU.mult), inc=sv)
            em("vector", lambda e, t2=t2, cre=cre, qim=qim: e.tensor_tensor(t2, cre[:, 1:512], qim, ALU.mult), inc=sv)
            zl = em("vector", lambda e, zim=zim, t2=t2: e.tensor_tensor(zim, zim, t2, ALU.subtract), inc=sv)
            ctr = None
            for cch in range(8):
                ctr = em("tensor", lambda e, cch=cch, zbuf=zbuf: e.transpose(
                    PTb[:, cch * 128:(cch + 1) * 128], zbuf[:, cch * 128:(cch + 1) * 128], mI),
                    waits=([(sv, zl)] + ([(sv, ztc_prev)] if ztc_prev is not None else [])) if cch == 0 else [],
                    inc=st if cch == 7 else None)
            ctr_prev = ctr
            ztc = em("vector", lambda e: e.tensor_copy(ztb, PTb[:, 0:1024]), waits=[(st, ctr)], inc=sv)
            ztc_prev = ztc
            w2 = [(sv, ztc), (d1, cab)]
            for hlf in range(2):
                for cch in range(8):
                    em("tensor", lambda e, ps=ps, cch=cch, hlf=hlf: e.matmul(
                        ps[:, hlf * 512:(hlf + 1) * 512],
                        ztb[:, cch * 128:(cch + 1) * 128],
                        wBIG[:, cch * 1024 + hlf * 512: cch * 1024 + (hlf + 1) * 512],
                        start=(cch == 0), stop=False),
                        waits=w2 if (cch == 0 and hlf == 0) else [])
                for cc2 in range(4):
                    cch = hlf * 4 + cc2
                    em("tensor", lambda e, ps=ps, cch=cch, t=t: e.matmul(
                        ps[:, cch * 128:(cch + 1) * 128],
                        xnT[:, cch * S + t * 128: cch * S + t * 128 + 128],
                        mI, start=False, stop=(cch % 4 == 3)),
                        inc=st if cch == 7 else None)
            evs2[t % 2] = residual_add(ps, t, [(st, st.n)])

    def lncheck():
        ln_xnT(double=False)
        wait("sync", [(sv, sv.n)])
        em("sync", lambda e: e.dma_start(out=dbg, in_=xnT), inc=d1, inc_by=16)
        em("sync", lambda e: e.dma_start(out=dbg2, in_=misc32), inc=d1, inc_by=16)

    def lncheck2():
        ln_xnT(double=True)
        wait("sync", [(sv, sv.n)])
        em("sync", lambda e: e.dma_start(out=dbg, in_=xnT), inc=d1, inc_by=16)

    sub_map = {"attn_a": attn_a, "ffn": ffn, "attn_b": attn_b, "lncheck": lncheck, "lncheck2": lncheck2,
               "qk0": lambda: attn_a(dbg_stop=True), "attn0": lambda: attn_a(dbg_stop2=True)}
    for be in range(n_be):
        wait("sync", [(sv, sv.n), (ss, ss.n), (st, st.n)])
        cx = None
        for t in range(NT):
            cx = em("sync", lambda e, be=be, t=t: e.dma_start(
                out=xres[:, t * D:(t + 1) * D], in_=x_in[be, t * 128:(t + 1) * 128, :]),
                inc=d1, inc_by=16)
        for eng in ("tensor", "scalar", "vector", "gpsimd"):
            wait(eng, [(d1, d1.n), (d2, d2.n)])
        for name in sublayers:
            sub_map[name]()
        wait("sync", [(sv, sv.n)])
        cso = None
        for t in range(NT):
            cso = em("sync", lambda e, be=be, t=t: e.dma_start(
                out=x_out[be, t * 128:(t + 1) * 128, :], in_=xres[:, t * D:(t + 1) * D]),
                inc=d1, inc_by=16)
        wait("sync", [(d1, cso)])
    P.finalize(stack)
    stack.close()
    return nc


def kernel(**inputs):
    x = np.ascontiguousarray(inputs["L0_attn_a_x"], dtype=np.float32)
    n_cores = 8
    shards = np.split(x, n_cores, axis=0)
    nc = build_program()
    base = {
        "w_q": inputs["L0_attn_a_w_q"], "w_k": inputs["L0_attn_a_w_k"],
        "w_v": inputs["L0_attn_a_w_v"], "w_o": inputs["L0_attn_a_w_o"],
        "ln_g": inputs["L0_attn_a_ln_gamma"], "ln_b": inputs["L0_attn_a_ln_beta"],
        "q_g": inputs["L0_attn_a_q_gamma"], "k_g": inputs["L0_attn_a_k_gamma"],
        "b_o": inputs["L0_attn_a_b_o"],
        "bWq": inputs["L0_attn_b_Wq"], "bWk": inputs["L0_attn_b_Wk"], "bWv": inputs["L0_attn_b_Wv"],
        "Wsh": inputs["L0_ffn_W_shared"], "Wex": inputs["L0_ffn_W_expert"],
        "fbias": inputs["L0_ffn_bias"],
    }
    base = {k: np.ascontiguousarray(v, dtype=np.float32) for k, v in base.items()}
    in_maps = [dict(base, x_in=np.ascontiguousarray(s)) for s in shards]
    res = run_bass_kernel_spmd(nc, in_maps, core_ids=list(range(n_cores)))
    out = np.concatenate([np.asarray(r["x_out"]) for r in res.results], axis=0)
    return out.astype(np.float32)


# version 08
# speedup vs baseline: 1.0074x; 1.0074x over previous
import numpy as np
import ml_dtypes
import concourse.bass as bass
import concourse.mybir as mybir
from concourse.bass_utils import run_bass_kernel_spmd

F32 = mybir.dt.float32
BF16 = mybir.dt.bfloat16
AF = mybir.ActivationFunctionType
ALU = mybir.AluOpType
AX = mybir.AxisListType

D = 1024
S = 2048
NT = 16
H = 16
DH = 64
NPAIR = 8
NBE = 4
LN_EPS = 1e-5
RMS_EPS2 = 6.4e-15
KF_EPS = 1e-8
LN8 = float(np.log(8.0))
CONSTS = [LN_EPS, KF_EPS, 0.5, -0.5, -1.0, LN8]
BF = ml_dtypes.bfloat16


def _np_consts():
    e = np.arange(D)[:, None]
    F = np.zeros((D, D), np.float64)
    F[:, 0:513] = np.cos(2 * np.pi * e * np.arange(513)[None, :] / D)
    F[:, 513:1024] = -np.sin(2 * np.pi * e * np.arange(1, 512)[None, :] / D)
    i_ = np.arange(D)[None, :]
    AB = np.zeros((D, D), np.float64)
    w = np.full((513, 1), 2.0); w[0, 0] = 1.0; w[512, 0] = 1.0
    AB[0:513, :] = w * np.cos(2 * np.pi * np.arange(513)[:, None] * i_ / D) / D
    AB[513:1024, :] = -2.0 * np.sin(2 * np.pi * np.arange(1, 512)[:, None] * i_ / D) / D
    LT = (np.arange(128)[:, None] <= np.arange(128)[None, :]).astype(np.float64)
    ST = np.zeros((16, 16 * 128), np.float64)
    for t in range(16):
        ST[:t, t * 128:(t + 1) * 128] = 1.0
    I128 = np.eye(128)
    ONES = np.ones((128, 512))
    SEL2 = np.zeros((34, 128))
    for m in range(128):
        SEL2[m // 64, m] = 1.0
        SEL2[32 + m // 64, m] = 1.0
    BLK2 = np.zeros((128, 2))
    BLK2[0:64, 0] = 1.0; BLK2[64:128, 1] = 1.0
    cast = lambda a: np.ascontiguousarray(a.astype(BF))
    return dict(F=cast(F), AB=cast(AB), LT=cast(LT), ST=cast(ST), I128=cast(I128),
                ONES=cast(ONES), SEL2=cast(SEL2), BLK2=cast(BLK2))


class Sem:
    def __init__(self, h):
        self.h = h
        self.n = 0
    def bump(self, k):
        self.n += k
        return self.n


class Prog:
    def __init__(self):
        self.nc = bass.Bass("TRN2", target_bir_lowering=False, debug=False)
        self.tape = []

    def em(self, eng, fn, waits=(), inc=None, inc_by=1):
        w = tuple(waits)
        post = inc.bump(inc_by) if inc is not None else None
        def run(e, fn=fn, w=w, inc=inc, inc_by=inc_by):
            for s, v in w:
                e.wait_ge(s.h, v)
            ins = fn(e)
            if inc is not None:
                ins.then_inc(inc.h, inc_by)
        self.tape.append((eng, run))
        return post

    def wait(self, eng, waits):
        w = tuple(waits)
        def run(e, w=w):
            for s, v in w:
                e.wait_ge(s.h, v)
        self.tape.append((eng, run))

    def finalize(self, stack):
        block = stack.enter_context(self.nc.Block())
        tape = self.tape
        def make(engname):
            def body(e):
                for eng, fn in tape:
                    if eng == engname:
                        fn(e)
            return body
        block.sync(make("sync"))
        block.gpsimd(make("gpsimd"))
        block.tensor(make("tensor"))
        block.scalar(make("scalar"))
        block.vector(make("vector"))


def build_program(sublayers=("attn_a", "ffn", "attn_b", "ffn"), n_be=NBE, parts=("wq", "wk", "wv", "wo", "wc", "wfq", "wfk", "wfv")):
    import contextlib
    P = Prog()
    nc = P.nc
    stack = contextlib.ExitStack()
    em, wait = P.em, P.wait

    x_in = nc.dram_tensor("x_in", [n_be, S, D], F32, kind="ExternalInput").ap()
    x_out = nc.dram_tensor("x_out", [n_be, S, D], F32, kind="ExternalOutput").ap()
    dbg = nc.dram_tensor("dbg", [128, 16384], BF16, kind="ExternalOutput").ap()
    dbg2 = nc.dram_tensor("dbg2", [128, 1536], F32, kind="ExternalOutput").ap()
    wi = {}
    for nm, shape in [("w_q", [D, D]), ("w_k", [D, D]), ("w_v", [D, D]), ("w_o", [D, D]),
                      ("ln_g", [D]), ("ln_b", [D]), ("q_g", [DH]), ("k_g", [DH]), ("b_o", [D]),
                      ("bWq", [D, D]), ("bWk", [D, D]), ("bWv", [D, D]),
                      ("Wsh", [D, D]), ("Wex", [D, D]), ("fbias", [D])]:
        wi[nm] = nc.dram_tensor(nm, shape, F32, kind="ExternalInput").ap()

    C = _np_consts()
    ch = {k: nc.inline_tensor(v, k + "_c").ap() for k, v in C.items()}
    cn_np = np.zeros((128, 16), np.float32)
    for i, v in enumerate(CONSTS):
        cn_np[:, i] = v
    cn_h = nc.inline_tensor(np.ascontiguousarray(cn_np), "cn_c").ap()

    scr = {}
    for nm in ["wqT", "wkT", "wvT", "woT", "wcT", "wqF", "wkF", "wvF", "wnat"]:
        scr[nm] = nc.dram_tensor(nm + "_s", [D, D], BF16, kind="Internal").ap()
    pc = lambda a: a.rearrange("(c p) d -> p c d", p=128)
    v3 = lambda a, d=1024: a.rearrange("p (c d) -> p c d", d=d)

    A = lambda n, sh, dt: nc.alloc_sbuf_tensor(n, sh, dt).ap()
    xres = A("xres", [128, NT * D], F32)
    xnT = A("xnT", [128, 8 * S], BF16)
    aB = A("aB", [128, 16384], BF16)
    aC = A("aC", [128, 16384], BF16)
    aW = A("aW", [128, 8192], BF16)
    miscb = A("miscb", [128, 3968], BF16)
    misc32 = A("misc32", [128, 1536], F32)
    totals = A("totals", [16, 1024], BF16)
    sel2 = A("sel2", [34, 128], BF16)
    stbuf = A("stbuf", [16, 2048], BF16)
    bvrow = A("bvrow", [1, 1024], BF16)
    ztb = A("ztb", [128, 1024], BF16)
    trash = A("trash", [128, 1024], BF16)

    Pp = lambda n, sh, dt=F32: nc.alloc_psum_tensor(n, sh, dt).ap()
    P1 = Pp("P1", [128, 2048])
    P2 = Pp("P2", [128, 1024])
    P4 = Pp("P4", [128, 1024])
    PTb = P4.bitcast(BF16)

    mI = miscb[:, 0:128]
    mLT = miscb[:, 128:256]
    mBLK2 = miscb[:, 256:258]
    mONE = miscb[:, 384:896]
    mBO = miscb[:, 896:1920]
    mFB = miscb[:, 1920:2944]
    mZT = miscb[:, 2944:3968]

    sSUM = misc32[:, 0:16]
    sSSQ = misc32[:, 16:32]
    sM1 = misc32[:, 32:48]
    sV1 = misc32[:, 48:64]
    sR1 = misc32[:, 64:80]
    sR12 = misc32[:, 80:96]
    sT16 = misc32[:, 96:112]
    sGQ = misc32[:, 112:113]
    sGK = misc32[:, 113:114]
    sGL = misc32[:, 120:128]
    sCN = misc32[:, 152:168]
    sROW = misc32[:, 512:1536]

    sBETAb = A("sbetab", [128, 8], BF16)

    for i, v in enumerate(CONSTS):
        nc.const_aps.register(F32, v, sCN[:, i:i + 1])

    cQT = aC[:, 0:2048]
    cQN = aC[:, 2048:4096]
    cKT = aC[:, 4096:6144]
    cKN = aC[:, 6144:8192]
    cVA = aC[:, 8192:12288]
    cPT = [[aC[:, 0:1024], aC[:, 1024:2048]],
           [aC[:, 4096:5120], aC[:, 5120:6144]]]
    bcrr = [aC[:, 12288 + k * 1024: 13312 + k * 1024] for k in range(4)]
    wSL = [aW[:, k * 2048:(k + 1) * 2048] for k in range(3)]
    wBIG = aW[:, 0:8192]

    d1 = Sem(stack.enter_context(nc.semaphore("d1")))
    d2 = Sem(stack.enter_context(nc.semaphore("d2")))
    st = Sem(stack.enter_context(nc.semaphore("st")))
    ss = Sem(stack.enter_context(nc.semaphore("ss")))
    sv = Sem(stack.enter_context(nc.semaphore("sv")))

    for dst, src in [(mI, ch["I128"]), (mLT, ch["LT"]), (mONE, ch["ONES"]),
                     (mBLK2, ch["BLK2"]), (sel2, ch["SEL2"]),
                     (stbuf, ch["ST"]), (sCN, cn_h)]:
        em("sync", lambda e, dst=dst, src=src: e.dma_start(out=dst, in_=src), inc=d1, inc_by=16)
    em("sync", lambda e: e.dma_start(out=sGL, in_=wi["ln_g"].rearrange("(c p) -> p c", p=128), allow_slow_non_contiguous=True), inc=d1, inc_by=16)
    em("sync", lambda e: e.dma_start(out=sGQ[0:64, :], in_=wi["q_g"].unsqueeze(1), allow_slow_non_contiguous=True), inc=d1, inc_by=16)
    em("sync", lambda e: e.dma_start(out=sGQ[64:128, :], in_=wi["q_g"].unsqueeze(1), allow_slow_non_contiguous=True), inc=d1, inc_by=16)
    em("sync", lambda e: e.dma_start(out=sGK[0:64, :], in_=wi["k_g"].unsqueeze(1), allow_slow_non_contiguous=True), inc=d1, inc_by=16)
    em("sync", lambda e: e.dma_start(out=sGK[64:128, :], in_=wi["k_g"].unsqueeze(1), allow_slow_non_contiguous=True), inc=d1, inc_by=16)
    em("gpsimd", lambda e: e.dma_start(out=mBO[0:1, :], in_=wi["b_o"].unsqueeze(0)), inc=d2, inc_by=16)
    em("gpsimd", lambda e: e.dma_start(out=mFB[0:1, :], in_=wi["fbias"].unsqueeze(0)), inc=d2, inc_by=16)
    c_beta = em("gpsimd", lambda e: e.dma_start(out=sBETAb, in_=wi["ln_b"].rearrange("(c p) -> p c", p=128), allow_slow_non_contiguous=True),
                inc=d2, inc_by=16)
    consts_d1 = d1.n

    wnatb = aB[:, 0:8192]
    wTb = aC[:, 0:8192]
    fpk = wBIG

    def build_w(name_in, out_T=None, gamma=False, out_F=None, add_in=None, bias_row_dst=None, bias_pos=0):
        c1 = em("gpsimd", lambda e: e.dma_start(out=v3(wnatb), in_=pc(wi[name_in])),
                waits=[(d1, d1.n), (st, st.n), (sv, sv.n)], inc=d2, inc_by=16)
        dep = (d2, c1)
        if add_in is not None:
            c2 = em("gpsimd", lambda e: e.dma_start(out=v3(wTb), in_=pc(wi[add_in])), inc=d2, inc_by=16)
            cv = em("vector", lambda e: e.tensor_tensor(wnatb, wnatb, wTb, ALU.add),
                    waits=[(d2, c2)], inc=sv)
            dep = (sv, cv)
        if out_T is not None:
            c_nat = em("sync", lambda e: e.dma_start(out=pc(scr["wnat"]), in_=v3(wnatb)), waits=[dep], inc=d1, inc_by=16)
            tr = None
            for c in range(8):
                tr = em("sync", lambda e, c=c: e.dma_start(out=wTb[:, c * 1024:(c + 1) * 1024],
                                                           in_=scr["wnat"][:, c * 128:(c + 1) * 128],
                                                           transpose=True),
                        waits=[(d1, c_nat)] if c == 0 else [], inc=d1, inc_by=16)
            if bias_row_dst is not None:
                bp = bias_pos
                for half in range(2):
                    for k in range(8):
                        em("tensor", lambda e, half=half, k=k, bp=bp: e.matmul(
                            P2[bp:bp + 1, half * 512:(half + 1) * 512],
                            sBETAb[:, k:k + 1],
                            wTb[:, k * 1024 + half * 512: k * 1024 + (half + 1) * 512],
                            start=(k == 0), stop=(k == 7),
                            tile_position=(0, bp) if bp else None),
                            waits=[(d1, tr), (d2, c_beta), (ss, ss.n)] if (half == 0 and k == 0) else [],
                            inc=st if (half == 1 and k == 7) else None)
                em("scalar", lambda e, bias_row_dst=bias_row_dst, bp=bp: e.activation(
                    out=bias_row_dst, in_=P2[bp:bp + 1, :], func=AF.Copy), waits=[(st, st.n)], inc=ss)
            if gamma:
                for c in range(8):
                    em("vector", lambda e, c=c: e.tensor_scalar_mul(wTb[:, c * 1024:(c + 1) * 1024],
                                                                    wTb[:, c * 1024:(c + 1) * 1024],
                                                                    sGL[:, c:c + 1]),
                       waits=[(st, st.n), (d1, tr), (d1, consts_d1)] if c == 0 else [], inc=sv)
                dep2 = (sv, sv.n)
            else:
                dep2 = (d1, tr)
            em("sync", lambda e, out_T=out_T: e.dma_start(out=pc(scr[out_T]), in_=v3(wTb)),
               waits=[dep2, (sv, sv.n)], inc=d1, inc_by=16)
        if out_F is not None:
            cf = em("sync", lambda e: e.dma_start(out=v3(fpk), in_=pc(ch["F"])), inc=d1, inc_by=16)
            ev = None
            for dsl in range(8):
                for ec in range(8):
                    for half in range(2):
                        if ec == 0 and half == 0:
                            w_ = [(d1, cf), dep] if dsl == 0 else [(sv, ev)]
                        else:
                            w_ = []
                        em("tensor", lambda e, dsl=dsl, ec=ec, half=half: e.matmul(
                            P2[:, half * 512:(half + 1) * 512],
                            wnatb[:, ec * 1024 + dsl * 128: ec * 1024 + dsl * 128 + 128],
                            fpk[:, ec * 1024 + half * 512: ec * 1024 + half * 512 + 512],
                            start=(ec == 0), stop=(ec == 7)),
                            waits=w_,
                            inc=st if (ec == 7 and half == 1) else None)
                cst = st.n
                ev = em("vector", lambda e, dsl=dsl: e.tensor_copy(wTb[:, dsl * 1024:(dsl + 1) * 1024], P2),
                        waits=[(st, cst)], inc=sv)
            em("sync", lambda e, out_F=out_F: e.dma_start(out=pc(scr[out_F]), in_=v3(wTb)),
               waits=[(sv, ev)], inc=d1, inc_by=16)

    if "wq" in parts: build_w("w_q", out_T="wqT", gamma=True, bias_row_dst=mBO[32:33, :], bias_pos=32)
    if "wk" in parts: build_w("w_k", out_T="wkT", gamma=True, bias_row_dst=mBO[64:65, :], bias_pos=64)
    if "wv" in parts: build_w("w_v", out_T="wvT", gamma=True, bias_row_dst=bvrow)
    if "wo" in parts: build_w("w_o", out_T="woT")
    if "wc" in parts: build_w("Wsh", out_T="wcT", add_in="Wex")
    if "wfq" in parts: build_w("bWq", out_F="wqF")
    if "wfk" in parts: build_w("bWk", out_F="wkF")
    if "wfv" in parts: build_w("bWv", out_F="wvF")

    def ln_xnT(double):
        for t in range(NT):
            xt = xres[:, t * D:(t + 1) * D]
            em("scalar", lambda e, xt=xt, t=t: e.activation(out=trash, in_=xt, func=AF.Square,
                                                            accum_out=sSSQ[:, t:t + 1]),
               waits=[(sv, add_sv[t])], inc=ss)
            em("vector", lambda e, xt=xt, t=t: e.reduce_sum(sSUM[:, t:t + 1], xt, axis=AX.X),
               waits=[(sv, add_sv[t])] if False else [], inc=sv)
        cs_stats, cv_stats = ss.n, sv.n
        c_ = em("vector", lambda e: e.tensor_scalar_mul(sM1, sSUM, 1.0 / D), waits=[(sv, cv_stats)], inc=sv)
        c_ = em("vector", lambda e: e.scalar_tensor_tensor(sT16, sM1, 1.0, sM1, ALU.mult, ALU.mult),
                waits=[(sv, c_)], inc=sv)
        c_ = em("vector", lambda e: e.tensor_scalar_mul(sV1, sSSQ, 1.0 / D),
                waits=[(ss, cs_stats), (sv, c_)], inc=sv)
        cv = em("vector", lambda e: e.tensor_tensor(sV1, sV1, sT16, ALU.subtract), waits=[(sv, c_)], inc=sv)
        em("scalar", lambda e: e.activation(out=sT16, in_=sV1, func=AF.Ln, bias=LN_EPS), waits=[(sv, cv)], inc=ss)
        cr1 = em("scalar", lambda e: e.activation(out=sR1, in_=sT16, func=AF.Exp, scale=-0.5), inc=ss)
        if double:
            c_ = em("vector", lambda e: e.scalar_tensor_tensor(sT16, sR1, 1.0, sR1, ALU.mult, ALU.mult),
                    waits=[(ss, cr1)], inc=sv)
            cv = em("vector", lambda e: e.tensor_tensor(sT16, sV1, sT16, ALU.mult), waits=[(sv, c_)], inc=sv)
            em("scalar", lambda e: e.activation(out=sT16, in_=sT16, func=AF.Ln, bias=LN_EPS),
               waits=[(sv, cv)], inc=ss)
            cr2 = em("scalar", lambda e: e.activation(out=sT16, in_=sT16, func=AF.Exp, scale=-0.5), inc=ss)
            cr12 = em("vector", lambda e: e.tensor_tensor(sR12, sR1, sT16, ALU.mult),
                      waits=[(ss, cr2), (sv, cv)], inc=sv)
            rfin, rsem, rcnt = sR12, sv, cr12
        else:
            rfin, rsem, rcnt = sR1, ss, cr1
        for t in range(NT):
            xt = xres[:, t * D:(t + 1) * D]
            cv_ap = em("vector", lambda e, xt=xt, t=t, rfin=rfin: e.tensor_scalar(
                mZT, xt, sM1[:, t:t + 1], rfin[:, t:t + 1], ALU.subtract, ALU.mult),
                waits=([(rsem, rcnt), (st, st.n), (ss, cs_stats)] if t == 0 else [(st, st.n)]), inc=sv)
            ctr = None
            for cch in range(8):
                ctr = em("tensor", lambda e, cch=cch: e.transpose(PTb[:, cch * 128:(cch + 1) * 128],
                                                                  mZT[:, cch * 128:(cch + 1) * 128], mI),
                         waits=[(sv, cv_ap)] if cch == 0 else [], inc=st if cch == 7 else None)
            em("vector", lambda e, t=t: e.tensor_copy(
                xnT.rearrange("p (c s) -> p c s", c=8)[:, :, t * 128:(t + 1) * 128], PTb[:, 0:1024].rearrange("p (c s) -> p c s", c=8)),
               waits=[(st, ctr)], inc=sv)

    add_sv = [0] * NT
    def residual_add(psum_ap, t, waits):
        xt = xres[:, t * D:(t + 1) * D]
        c = em("vector", lambda e, xt=xt, psum_ap=psum_ap: e.tensor_tensor(xt, xt, psum_ap, ALU.add),
               waits=waits, inc=sv)
        add_sv[t] = c
        return c

    def dense_out(wname, bias_row, src_T, wload_waits):
        cwh = em("sync", lambda e, wname=wname: e.dma_start(out=v3(wBIG)[:, 0:4, :], in_=pc(scr[wname])[:, 0:4, :]),
                 waits=wload_waits, inc=d1, inc_by=16)
        cw = em("sync", lambda e, wname=wname: e.dma_start(out=v3(wBIG)[:, 4:8, :], in_=pc(scr[wname])[:, 4:8, :]),
                inc=d1, inc_by=16)
        sv0, ss0 = sv.n, ss.n
        prot = [P2, P1[:, 0:1024], P1[:, 1024:2048]]
        evs = [None, None, None]
        for t in range(NT):
            ps = prot[t % 3]
            w_ = [(d1, cw)] if t == 0 else []
            if t < 3:
                w_.append((sv, sv0)); w_.append((ss, ss0))
            if evs[t % 3] is not None:
                w_.append((sv, evs[t % 3]))
            w_ = [x for x in w_ if x[1] is not None]
            w_ = ([(d1, cwh)] if t == 0 else []) + [x for x in w_ if x[0] is not d1]
            for hlf in range(2):
                em("tensor", lambda e, ps=ps, hlf=hlf, bias_row=bias_row: e.matmul(
                    ps[:, hlf * 512:(hlf + 1) * 512], mONE[0:1, 0:128],
                    bias_row[0:1, hlf * 512:(hlf + 1) * 512], start=True, stop=False),
                    waits=w_ if hlf == 0 else [])
            for cch in range(8):
                for hlf in range(2):
                    em("tensor", lambda e, ps=ps, cch=cch, hlf=hlf, t=t, src_T=src_T: e.matmul(
                        ps[:, hlf * 512:(hlf + 1) * 512],
                        src_T[:, cch * S + t * 128: cch * S + t * 128 + 128],
                        wBIG[:, cch * 1024 + hlf * 512: cch * 1024 + (hlf + 1) * 512],
                        start=False, stop=(cch == 7)),
                        waits=[(d1, cw)] if (t == 0 and cch == 4 and hlf == 0) else [],
                        inc=st if (cch == 7 and hlf == 1) else None)
            evs[t % 3] = residual_add(ps, t, [(st, st.n)])
        return evs

    def ffn():
        ln_xnT(double=False)
        dense_out("wcT", mFB, xnT, [(st, st.n)])

    def attn_a(dbg_stop=False, dbg_stop2=False):
        ln_xnT(double=True)
        em("vector", lambda e: e.memset(cVA.rearrange("p (t c) -> p t c", c=256)[:, :, 64:192], 1.0),
           waits=[(st, st.n), (ss, ss.n)], inc=sv)

        def stream_slices(g, buf, waits=()):
            cnt = None
            for mi, nm in enumerate(["wqT", "wkT", "wvT"]):
                cnt = em("sync", lambda e, nm=nm, g=g, mi=mi, buf=buf: e.dma_start(
                    out=v3(wSL[mi][:, buf * 1024:(buf + 1) * 1024], d=128),
                    in_=pc(scr[nm][:, g * 128:(g + 1) * 128])),
                    waits=waits if mi == 0 else (), inc=d1, inc_by=16)
            return cnt

        slice_cw = [None, None]
        slice_cw[0] = stream_slices(0, 0)
        pair_done_sv = None
        for g in range(NPAIR):
            buf = g % 2
            if g + 1 < NPAIR:
                slice_cw[(g + 1) % 2] = stream_slices(g + 1, (g + 1) % 2, waits=[(st, st.n)])
            sl = lambda mi, buf=buf: wSL[mi][:, buf * 1024:(buf + 1) * 1024]

            pair_sv, pair_ss = sv.n, ss.n

            def proj_mm(mi, bp, w0, sqbuf):
                for q4 in range(4):
                    em("tensor", lambda e, q4=q4, bp=bp, g=g: e.matmul(
                        P1[:, q4 * 512:(q4 + 1) * 512],
                        mBO[bp:bp + 1, g * 128:(g + 1) * 128],
                        mONE[bp:bp + 1, 0:512], start=True, stop=False),
                        waits=w0 if q4 == 0 else [])
                slap = sl(mi)
                for cch in range(8):
                    for q4 in range(4):
                        em("tensor", lambda e, cch=cch, q4=q4, slap=slap: e.matmul(
                            P1[:, q4 * 512:(q4 + 1) * 512],
                            slap[:, cch * 128:(cch + 1) * 128],
                            xnT[:, cch * S + q4 * 512: cch * S + (q4 + 1) * 512],
                            start=False, stop=(cch == 7)),
                            inc=st if (cch == 7 and q4 == 3) else None)
                cm = st.n
                return em("scalar", lambda e, sqbuf=sqbuf: e.activation(out=sqbuf, in_=P1, func=AF.Square),
                          waits=[(st, cm)], inc=ss)

            def chain_mm(c1, gcol, sqbuf, nrm, kside, p2_sv, p2_ss, filler=None):
                cqn = None
                last_cev = None
                for h in range(2):
                    wssq = [(ss, c1), (sv, p2_sv)] + ([(ss, p2_ss)] if last_cev is None else [(ss, last_cev)])
                    em("tensor", lambda e, h=h, sqbuf=sqbuf: e.matmul(
                        P2[0:2, 0:512], mBLK2, sqbuf[:, h * 1024: h * 1024 + 512],
                        start=True, stop=True), waits=wssq)
                    em("tensor", lambda e, h=h, sqbuf=sqbuf: e.matmul(
                        P2[0:2, 512:1024], mBLK2, sqbuf[:, h * 1024 + 512: (h + 1) * 1024],
                        start=True, stop=True), inc=st)
                    cssq = st.n
                    if filler is not None:
                        filler()
                    cvm = em("vector", lambda e: e.tensor_scalar_max(P2[0:2, :], P2[0:2, :], RMS_EPS2),
                             waits=[(st, cssq)], inc=sv)
                    cl = em("scalar", lambda e: e.activation(out=P2[0:2, :], in_=P2[0:2, :], func=AF.Ln),
                            waits=[(sv, cvm)], inc=ss)
                    if kside:
                        cex = em("scalar", lambda e: e.activation(out=sROW[0:2, :], in_=P2[0:2, :],
                                                                  func=AF.Exp, scale=-0.5, bias=LN8),
                                 waits=[(ss, cl)], inc=ss)
                    else:
                        cex = em("scalar", lambda e: e.activation(out=sROW[0:2, :], in_=P2[0:2, :],
                                                                  func=AF.Exp, scale=-0.5),
                                 waits=[(ss, cl)], inc=ss)
                    chi = em("scalar", lambda e: e.activation(out=cPT[0][0][0:2, :], in_=sROW[0:2, :],
                                                              func=AF.Copy), waits=[(ss, cex)], inc=ss)
                    clo = em("vector", lambda e: e.tensor_tensor(cPT[0][1][0:2, :], sROW[0:2, :],
                                                                 cPT[0][0][0:2, :], ALU.subtract),
                             waits=[(ss, chi)], inc=sv)
                    for q2 in range(2):
                        em("tensor", lambda e, q2=q2: e.matmul(
                            P2[:, q2 * 512:(q2 + 1) * 512], sel2[0:2, :],
                            cPT[0][0][0:2, q2 * 512:(q2 + 1) * 512], start=True, stop=False),
                            waits=[(sv, clo)] if q2 == 0 else [])
                        em("tensor", lambda e, q2=q2: e.matmul(
                            P2[:, q2 * 512:(q2 + 1) * 512], sel2[0:2, :],
                            cPT[0][1][0:2, q2 * 512:(q2 + 1) * 512], start=False, stop=True),
                            inc=st if q2 == 1 else None)
                    cbc = st.n
                    cev = em("scalar", lambda e: e.activation(out=sROW, in_=P2, func=AF.Copy),
                             waits=[(st, cbc)], inc=ss)
                    last_cev = cev
                    cqn = em("vector", lambda e, h=h, nrm=nrm, gcol=gcol: e.scalar_tensor_tensor(
                        nrm[:, h * 1024:(h + 1) * 1024], P1[:, h * 1024:(h + 1) * 1024], gcol,
                        sROW, ALU.mult, ALU.mult), waits=[(ss, cev)], inc=sv)
                return cqn

            def vproj_range(t0_, t1_, vstate):
                for t in range(t0_, t1_):
                    slot = t % 8
                    w_ = []
                    if t == 0:
                        w_ = [(sv, pair_sv), (ss, pair_ss)]
                    if t >= 8 and t % 8 == 0:
                        w_ = [(sv, vstate["cvev"])]
                    em("tensor", lambda e, slot=slot, g=g: e.matmul(
                        P4[:, slot * 128:(slot + 1) * 128], mONE[0:1, 0:128],
                        bvrow[0:1, g * 128:(g + 1) * 128], start=True, stop=False), waits=w_)
                    slap2 = sl(2)
                    for cch in range(8):
                        em("tensor", lambda e, t=t, cch=cch, slot=slot, slap2=slap2: e.matmul(
                            P4[:, slot * 128:(slot + 1) * 128],
                            xnT[:, cch * S + t * 128: cch * S + t * 128 + 128],
                            slap2[:, cch * 128:(cch + 1) * 128],
                            start=False, stop=(cch == 7)),
                            inc=st if cch == 7 else None)
                    if t % 8 == 7:
                        cpv = st.n
                        tb = t - 7
                        vA = cVA.rearrange("p (t x) -> p t x", x=256)[:, tb:tb + 8, 0:64]
                        vB = cVA.rearrange("p (t x) -> p t x", x=256)[:, tb:tb + 8, 192:256]
                        pA = P4.rearrange("p (t u c) -> p t u c", t=8, u=2, c=64)[:, :, 0, :]
                        pB = P4.rearrange("p (t u c) -> p t u c", t=8, u=2, c=64)[:, :, 1, :]
                        em("vector", lambda e, vA=vA, pA=pA: e.tensor_copy(vA, pA),
                           waits=[(st, cpv)], inc=sv)
                        vstate["cvev"] = em("vector", lambda e, vB=vB, pB=pB: e.tensor_copy(vB, pB), inc=sv)

            vstate = {"cvev": None, "next": 0}

            def vfill():
                t0_ = vstate["next"]
                if t0_ < NT:
                    vstate["next"] = t0_ + 4
                    vproj_range(t0_, t0_ + 4, vstate)

            w0 = [(d1, slice_cw[buf]), (sv, pair_sv), (ss, pair_ss)]
            c1q = proj_mm(0, 32, w0, cQT)
            cq = chain_mm(c1q, sGQ, cQT, cQN, False, pair_sv, pair_ss, filler=vfill)
            c1k = proj_mm(1, 64, [(sv, cq)], cKT)
            ck = chain_mm(c1k, sGK, cKT, cKN, True, pair_sv, pair_ss, filler=vfill)
            while vstate["next"] < NT:
                vfill()
            crms = ck
            cvev = vstate["cvev"]
            if dbg_stop:
                wait("sync", [(sv, sv.n), (ss, ss.n), (st, st.n)])
                em("sync", lambda e: e.dma_start(out=dbg[:, 0:2048], in_=cQN), inc=d1, inc_by=16)
                em("sync", lambda e: e.dma_start(out=dbg[:, 2048:4096], in_=cKN), inc=d1, inc_by=16)
                em("sync", lambda e: e.dma_start(out=dbg[:, 4096:6176], in_=cVA), inc=d1, inc_by=16)
                return
            PSIM = [P1[:, 0:1024], P1[:, 1024:2048]]
            PPV = [P2, P4]
            simev = [None, None]
            pvdone = [[None, None], [None, None]]
            otv_prev = [cvev, cvev]
            for ih in range(2):
                for jt in range(NT):
                    csims = [None, None]
                    for hh in range(2):
                        pb = 64 * hh
                        w_ = []
                        if simev[hh] is not None:
                            w_.append((ss, simev[hh]))
                        elif jt == 0:
                            w_.append((sv, crms))
                        for q2 in range(2):
                            em("tensor", lambda e, hh=hh, q2=q2, pb=pb, jt=jt, ih=ih: e.matmul(
                                PSIM[hh][:, q2 * 512:(q2 + 1) * 512],
                                cKN[pb:pb + 64, jt * 128:(jt + 1) * 128],
                                cQN[pb:pb + 64, ih * 1024 + q2 * 512: ih * 1024 + (q2 + 1) * 512],
                                start=True, stop=True,
                                tile_position=(pb, 0)),
                                waits=w_ if q2 == 0 else [],
                                inc=st if q2 == 1 else None)
                        csims[hh] = st.n
                    for hh in range(2):
                        wexp = [(st, csims[hh])]
                        if pvdone[hh][jt % 2] is not None:
                            wexp.append((st, pvdone[hh][jt % 2]))
                        simev[hh] = em("scalar", lambda e, hh=hh, jt=jt: e.activation(
                            out=cPT[hh][jt % 2], in_=PSIM[hh], func=AF.Exp), waits=wexp, inc=ss)
                    for hh in range(2):
                        aug = cVA[:, jt * 256 + hh * 128: jt * 256 + (hh + 1) * 128]
                        wpv = [(ss, simev[hh])]
                        if jt == 0:
                            wpv.append((sv, otv_prev[hh]))
                        for q2 in range(2):
                            em("tensor", lambda e, hh=hh, q2=q2, jt=jt, aug=aug: e.matmul(
                                PPV[hh][:, q2 * 512:(q2 + 1) * 512], aug,
                                cPT[hh][jt % 2][:, q2 * 512:(q2 + 1) * 512],
                                start=(jt == 0), stop=(jt == NT - 1)),
                                waits=wpv if q2 == 0 else [],
                                inc=st if q2 == 1 else None)
                        pvdone[hh][jt % 2] = st.n
                for hh in range(2):
                    i = hh
                    pv15 = pvdone[hh][(NT - 1) % 2]
                    reg = bcrr[2 * hh + ih]
                    if hh == 0:
                        rsrows, otrows = slice(64, 128), slice(0, 64)
                        dst = aB[0:64, g * S + ih * 1024: g * S + (ih + 1) * 1024]
                    else:
                        rsrows, otrows = slice(0, 64), slice(64, 128)
                        dst = aB[64:128, g * S + ih * 1024: g * S + (ih + 1) * 1024]
                    cl = em("scalar", lambda e, i=i, rsrows=rsrows: e.activation(
                        out=PPV[i][rsrows, :], in_=PPV[i][rsrows, :], func=AF.Ln),
                        waits=[(st, pv15)], inc=ss)
                    ce = em("scalar", lambda e, i=i, rsrows=rsrows, reg=reg: e.activation(
                        out=reg[rsrows, :], in_=PPV[i][rsrows, :], func=AF.Exp, scale=-1.0),
                        waits=[(ss, cl)], inc=ss)
                    cdm = em("sync", lambda e, reg=reg, rsrows=rsrows, otrows=otrows: e.dma_start(
                        out=reg[otrows, :], in_=reg[rsrows, :]), waits=[(ss, ce)], inc=d1, inc_by=16)
                    otv_prev[hh] = em("vector", lambda e, i=i, otrows=otrows, reg=reg, dst=dst: e.tensor_tensor(
                        dst, PPV[i][otrows, :], reg[otrows, :], ALU.mult),
                        waits=[(d1, cdm)], inc=sv)
            pair_done_sv = sv.n
            if dbg_stop2 and g == NPAIR - 1:
                wait("sync", [(sv, sv.n), (ss, ss.n), (st, st.n), (d1, d1.n)])
                em("sync", lambda e: e.dma_start(out=dbg, in_=aB), inc=d1, inc_by=16)
                return
        dense_out("woT", mBO, aB, [(st, st.n), (sv, sv.n)])

    def attn_b():
        ln_xnT(double=False)

        slotfree = {0: None, 1: None, 2: None}

        def fft_proj(nm, out_arena, extra0):
            cw = em("sync", lambda e, nm=nm: e.dma_start(out=v3(wBIG), in_=pc(scr[nm])),
                    waits=[(st, st.n)], inc=d1, inc_by=16)
            prot = [P2, P1[:, 0:1024], P1[:, 1024:2048]]
            evs = [None, None, None]
            last = None
            for t in range(NT):
                ps = prot[t % 3]
                w_ = ([(d1, cw)] + extra0) if t == 0 else []
                if t < 3 and slotfree[t % 3] is not None:
                    w_.append(slotfree[t % 3])
                if evs[t % 3] is not None:
                    w_.append((sv, evs[t % 3]))
                for cch in range(8):
                    for hlf in range(2):
                        em("tensor", lambda e, ps=ps, cch=cch, hlf=hlf, t=t: e.matmul(
                            ps[:, hlf * 512:(hlf + 1) * 512],
                            xnT[:, cch * S + t * 128: cch * S + t * 128 + 128],
                            wBIG[:, cch * 1024 + hlf * 512: cch * 1024 + (hlf + 1) * 512],
                            start=(cch == 0), stop=(cch == 7)),
                            waits=w_ if (cch == 0 and hlf == 0) else [],
                            inc=st if (cch == 7 and hlf == 1) else None)
                evs[t % 3] = em("vector", lambda e, ps=ps, t=t, out_arena=out_arena: e.tensor_copy(
                    out_arena[:, t * 1024:(t + 1) * 1024], ps), waits=[(st, st.n)], inc=sv)
                slotfree[t % 3] = (sv, evs[t % 3])
                last = evs[t % 3]
            return last

        cfv = fft_proj("wvF", aC, [])
        cwk = em("sync", lambda e: e.dma_start(out=v3(wBIG), in_=pc(scr["wkF"])), waits=[(st, st.n)], inc=d1, inc_by=16)
        prot = [P2, P1[:, 0:1024], P1[:, 1024:2048]]
        evps = [None, None, None]
        fkv_last = None
        for t in range(NT):
            ps = prot[t % 3]
            w_ = [(d1, cwk)] if t == 0 else []
            if t < 3 and slotfree[t % 3] is not None:
                w_.append(slotfree[t % 3])
            if t == 0:
                w_.append((sv, cfv))
            if evps[t % 3] is not None:
                w_.append((ss, evps[t % 3]))
            for cch in range(8):
                for hlf in range(2):
                    em("tensor", lambda e, ps=ps, cch=cch, hlf=hlf, t=t: e.matmul(
                        ps[:, hlf * 512:(hlf + 1) * 512],
                        xnT[:, cch * S + t * 128: cch * S + t * 128 + 128],
                        wBIG[:, cch * 1024 + hlf * 512: cch * 1024 + (hlf + 1) * 512],
                        start=(cch == 0), stop=(cch == 7)),
                        waits=w_ if (cch == 0 and hlf == 0) else [],
                        inc=st if (cch == 7 and hlf == 1) else None)
            cfk = st.n
            fkt = mZT
            c0 = em("scalar", lambda e, ps=ps: e.activation(out=fkt, in_=ps, func=AF.Copy),
                    waits=[(st, cfk), (sv, sv.n)], inc=ss)
            evps[t % 3] = c0
            slotfree[t % 3] = (ss, c0)
            tile = aC[:, t * 1024:(t + 1) * 1024]
            vre, vim = tile[:, 0:513], tile[:, 513:1024]
            kre, kim = fkt[:, 0:513], fkt[:, 513:1024]
            ss_ = sROW[:, 0:513]
            t2 = sROW[:, 513:1024]
            pbuf = aB[:, 15 * 1024:16 * 1024]
            ppre, ppim = pbuf[:, 0:513], pbuf[:, 513:1024]
            em("vector", lambda e, ss_=ss_, kre=kre: e.tensor_tensor(ss_, kre, kre, ALU.mult),
               waits=[(ss, c0)], inc=sv)
            em("vector", lambda e, t2=t2, kim=kim: e.tensor_tensor(t2, kim, kim, ALU.mult), inc=sv)
            c3 = em("vector", lambda e, ss_=ss_, t2=t2: e.tensor_tensor(ss_[:, 1:512], ss_[:, 1:512], t2, ALU.add), inc=sv)
            em("scalar", lambda e, ss_=ss_: e.activation(out=ss_, in_=ss_, func=AF.Ln), waits=[(sv, c3)], inc=ss)
            em("scalar", lambda e, ss_=ss_: e.activation(out=ss_, in_=ss_, func=AF.Exp, scale=0.5), inc=ss)
            em("scalar", lambda e, ss_=ss_: e.activation(out=ss_, in_=ss_, func=AF.Ln, bias=KF_EPS), inc=ss)
            c7 = em("scalar", lambda e, ss_=ss_: e.activation(out=ss_, in_=ss_, func=AF.Exp, scale=-1.0), inc=ss)
            em("vector", lambda e, ppre=ppre, kre=kre, vre=vre: e.tensor_tensor(ppre, kre, vre, ALU.mult), inc=sv)
            em("vector", lambda e, t2=t2, kim=kim, vim=vim: e.tensor_tensor(t2[:, 0:511], kim, vim, ALU.mult), inc=sv)
            em("vector", lambda e, ppre=ppre, t2=t2: e.tensor_tensor(ppre[:, 1:512], ppre[:, 1:512], t2[:, 0:511], ALU.subtract), inc=sv)
            em("vector", lambda e, ppim=ppim, kre=kre, vim=vim: e.tensor_tensor(ppim, kre[:, 1:512], vim, ALU.mult), inc=sv)
            em("vector", lambda e, t2=t2, kim=kim, vre=vre: e.tensor_tensor(t2[:, 0:511], kim, vre[:, 1:512], ALU.mult), inc=sv)
            em("vector", lambda e, ppim=ppim, t2=t2: e.tensor_tensor(ppim, ppim, t2[:, 0:511], ALU.add), inc=sv)
            em("vector", lambda e, vre=vre, ppre=ppre, ss_=ss_: e.tensor_tensor(vre, ppre, ss_, ALU.mult),
               waits=[(ss, c7)], inc=sv)
            fkv_last = em("vector", lambda e, vim=vim, ppim=ppim, ss_=ss_: e.tensor_tensor(
                vim, ppim, ss_[:, 1:512], ALU.mult), inc=sv)
        cfq = fft_proj("wqF", aB, [(sv, fkv_last)])
        cfqn = None
        for t in range(NT):
            tile = aB[:, t * 1024:(t + 1) * 1024]
            re, im = tile[:, 0:513], tile[:, 513:1024]
            ss_ = sROW[:, 0:513]
            t2 = sROW[:, 513:1024]
            em("vector", lambda e, ss_=ss_, re=re: e.tensor_tensor(ss_, re, re, ALU.mult),
               waits=[(sv, cfq)] if t == 0 else [], inc=sv)
            em("vector", lambda e, t2=t2, im=im: e.tensor_tensor(t2[:, 0:511], im, im, ALU.mult), inc=sv)
            c3 = em("vector", lambda e, ss_=ss_, t2=t2: e.tensor_tensor(ss_[:, 1:512], ss_[:, 1:512], t2[:, 0:511], ALU.add), inc=sv)
            em("scalar", lambda e, ss_=ss_: e.activation(out=ss_, in_=ss_, func=AF.Ln), waits=[(sv, c3)], inc=ss)
            em("scalar", lambda e, ss_=ss_: e.activation(out=ss_, in_=ss_, func=AF.Exp, scale=0.5), inc=ss)
            em("scalar", lambda e, ss_=ss_: e.activation(out=ss_, in_=ss_, func=AF.Ln, bias=KF_EPS), inc=ss)
            c7 = em("scalar", lambda e, ss_=ss_: e.activation(out=ss_, in_=ss_, func=AF.Exp, scale=-1.0), inc=ss)
            em("vector", lambda e, re=re, ss_=ss_: e.tensor_tensor(re, re, ss_, ALU.mult), waits=[(ss, c7)], inc=sv)
            cfqn = em("vector", lambda e, im=im, ss_=ss_: e.tensor_tensor(im, im, ss_[:, 1:512], ALU.mult), inc=sv)
        ctot = None
        ev_prev = None
        for t in range(NT):
            w_ = [(sv, cfqn)] if t == 0 else []
            if t == 0 and slotfree[0] is not None:
                w_.append(slotfree[0])
            if ev_prev is not None:
                w_.append((ss, ev_prev))
            for hlf in range(2):
                em("tensor", lambda e, t=t, hlf=hlf: e.matmul(
                    P2[:, hlf * 512:(hlf + 1) * 512], mLT,
                    aC[:, t * 1024 + hlf * 512: t * 1024 + (hlf + 1) * 512],
                    start=True, stop=True),
                    waits=w_ if hlf == 0 else [], inc=st if hlf == 1 else None)
            ccum = st.n
            ev_prev = em("scalar", lambda e, t=t: e.activation(out=aC[:, t * 1024:(t + 1) * 1024], in_=P2,
                                                               func=AF.Copy), waits=[(st, ccum)], inc=ss)
            ctot = em("sync", lambda e, t=t: e.dma_start(out=totals[t:t + 1, :],
                                                         in_=aC[127:128, t * 1024:(t + 1) * 1024]),
                      waits=[(ss, ev_prev)], inc=d1, inc_by=16)
        cab = em("sync", lambda e: e.dma_start(out=v3(wBIG), in_=pc(ch["AB"])), waits=[(st, st.n)], inc=d1, inc_by=16)
        prot2 = [P2, P1[:, 0:1024]]
        evs2 = [None, None]
        ctr_prev = None
        ztc_prev = None
        for t in range(NT):
            tile = aC[:, t * 1024:(t + 1) * 1024]
            ps = prot2[t % 2]
            w_ = [(d1, ctot)] if t == 0 else []
            if evs2[t % 2] is not None:
                w_.append((sv, evs2[t % 2]))
            if t > 0:
                for hlf in range(2):
                    em("tensor", lambda e, t=t, hlf=hlf, ps=ps: e.matmul(
                        ps[:, hlf * 512:(hlf + 1) * 512], stbuf[:, t * 128:(t + 1) * 128],
                        totals[:, hlf * 512:(hlf + 1) * 512], start=True, stop=True),
                        waits=w_ if hlf == 0 else [], inc=st if hlf == 1 else None)
                coff = st.n
                czadd = em("vector", lambda e, tile=tile, ps=ps: e.tensor_tensor(tile, tile, ps, ALU.add),
                           waits=[(st, coff)], inc=sv)
            else:
                wait("vector", w_)
                czadd = sv.n
            zbuf = mZT
            fq = aB[:, t * 1024:(t + 1) * 1024]
            cre, cim = tile[:, 0:513], tile[:, 513:1024]
            qre, qim = fq[:, 0:513], fq[:, 513:1024]
            zre, zim = zbuf[:, 0:513], zbuf[:, 513:1024]
            t2 = sROW[:, 0:511]
            em("vector", lambda e, zre=zre, cre=cre, qre=qre: e.tensor_tensor(zre, cre, qre, ALU.mult),
               waits=[(sv, czadd)] + ([(st, ctr_prev)] if ctr_prev is not None else []), inc=sv)
            em("vector", lambda e, t2=t2, cim=cim, qim=qim: e.tensor_tensor(t2, cim, qim, ALU.mult), inc=sv)
            em("vector", lambda e, zre=zre, t2=t2: e.tensor_tensor(zre[:, 1:512], zre[:, 1:512], t2, ALU.add), inc=sv)
            em("vector", lambda e, zim=zim, cim=cim, qre=qre: e.tensor_tensor(zim, cim, qre[:, 1:512], ALU.mult), inc=sv)
            em("vector", lambda e, t2=t2, cre=cre, qim=qim: e.tensor_tensor(t2, cre[:, 1:512], qim, ALU.mult), inc=sv)
            zl = em("vector", lambda e, zim=zim, t2=t2: e.tensor_tensor(zim, zim, t2, ALU.subtract), inc=sv)
            ctr = None
            for cch in range(8):
                ctr = em("tensor", lambda e, cch=cch, zbuf=zbuf: e.transpose(
                    PTb[:, cch * 128:(cch + 1) * 128], zbuf[:, cch * 128:(cch + 1) * 128], mI),
                    waits=([(sv, zl)] + ([(sv, ztc_prev)] if ztc_prev is not None else [])) if cch == 0 else [],
                    inc=st if cch == 7 else None)
            ctr_prev = ctr
            ztc = em("vector", lambda e: e.tensor_copy(ztb, PTb[:, 0:1024]), waits=[(st, ctr)], inc=sv)
            ztc_prev = ztc
            w2 = [(sv, ztc), (d1, cab)]
            for hlf in range(2):
                for cch in range(8):
                    em("tensor", lambda e, ps=ps, cch=cch, hlf=hlf: e.matmul(
                        ps[:, hlf * 512:(hlf + 1) * 512],
                        ztb[:, cch * 128:(cch + 1) * 128],
                        wBIG[:, cch * 1024 + hlf * 512: cch * 1024 + (hlf + 1) * 512],
                        start=(cch == 0), stop=False),
                        waits=w2 if (cch == 0 and hlf == 0) else [])
                for cc2 in range(4):
                    cch = hlf * 4 + cc2
                    em("tensor", lambda e, ps=ps, cch=cch, t=t: e.matmul(
                        ps[:, cch * 128:(cch + 1) * 128],
                        xnT[:, cch * S + t * 128: cch * S + t * 128 + 128],
                        mI, start=False, stop=(cch % 4 == 3)),
                        inc=st if cch == 7 else None)
            evs2[t % 2] = residual_add(ps, t, [(st, st.n)])

    def lncheck():
        ln_xnT(double=False)
        wait("sync", [(sv, sv.n)])
        em("sync", lambda e: e.dma_start(out=dbg, in_=xnT), inc=d1, inc_by=16)
        em("sync", lambda e: e.dma_start(out=dbg2, in_=misc32), inc=d1, inc_by=16)

    def lncheck2():
        ln_xnT(double=True)
        wait("sync", [(sv, sv.n)])
        em("sync", lambda e: e.dma_start(out=dbg, in_=xnT), inc=d1, inc_by=16)

    sub_map = {"attn_a": attn_a, "ffn": ffn, "attn_b": attn_b, "lncheck": lncheck, "lncheck2": lncheck2,
               "qk0": lambda: attn_a(dbg_stop=True), "attn0": lambda: attn_a(dbg_stop2=True)}
    for be in range(n_be):
        wait("sync", [(sv, sv.n), (ss, ss.n), (st, st.n)])
        cx = None
        for t in range(NT):
            cx = em("sync", lambda e, be=be, t=t: e.dma_start(
                out=xres[:, t * D:(t + 1) * D], in_=x_in[be, t * 128:(t + 1) * 128, :]),
                inc=d1, inc_by=16)
        for eng in ("tensor", "scalar", "vector", "gpsimd"):
            wait(eng, [(d1, d1.n), (d2, d2.n)])
        for name in sublayers:
            sub_map[name]()
        wait("sync", [(sv, sv.n)])
        cso = None
        for t in range(NT):
            cso = em("sync", lambda e, be=be, t=t: e.dma_start(
                out=x_out[be, t * 128:(t + 1) * 128, :], in_=xres[:, t * D:(t + 1) * D]),
                inc=d1, inc_by=16)
        wait("sync", [(d1, cso)])
    P.finalize(stack)
    stack.close()
    return nc


def kernel(**inputs):
    x = np.ascontiguousarray(inputs["L0_attn_a_x"], dtype=np.float32)
    n_cores = 8
    shards = np.split(x, n_cores, axis=0)
    nc = build_program()
    base = {
        "w_q": inputs["L0_attn_a_w_q"], "w_k": inputs["L0_attn_a_w_k"],
        "w_v": inputs["L0_attn_a_w_v"], "w_o": inputs["L0_attn_a_w_o"],
        "ln_g": inputs["L0_attn_a_ln_gamma"], "ln_b": inputs["L0_attn_a_ln_beta"],
        "q_g": inputs["L0_attn_a_q_gamma"], "k_g": inputs["L0_attn_a_k_gamma"],
        "b_o": inputs["L0_attn_a_b_o"],
        "bWq": inputs["L0_attn_b_Wq"], "bWk": inputs["L0_attn_b_Wk"], "bWv": inputs["L0_attn_b_Wv"],
        "Wsh": inputs["L0_ffn_W_shared"], "Wex": inputs["L0_ffn_W_expert"],
        "fbias": inputs["L0_ffn_bias"],
    }
    base = {k: np.ascontiguousarray(v, dtype=np.float32) for k, v in base.items()}
    in_maps = [dict(base, x_in=np.ascontiguousarray(s)) for s in shards]
    res = run_bass_kernel_spmd(nc, in_maps, core_ids=list(range(n_cores)))
    out = np.concatenate([np.asarray(r["x_out"]) for r in res.results], axis=0)
    return out.astype(np.float32)


# version 09 (final)
# speedup vs baseline: 1.0459x; 1.0382x over previous
import numpy as np
import ml_dtypes
import concourse.bass as bass
import concourse.mybir as mybir
from concourse.bass_utils import run_bass_kernel_spmd

F32 = mybir.dt.float32
BF16 = mybir.dt.bfloat16
AF = mybir.ActivationFunctionType
ALU = mybir.AluOpType
AX = mybir.AxisListType

D = 1024
S = 2048
NT = 16
H = 16
DH = 64
NPAIR = 8
NBE = 4
LN_EPS = 1e-5
RMS_EPS2 = 6.4e-15
KF_EPS = 1e-8
LN8 = float(np.log(8.0))
CONSTS = [LN_EPS, KF_EPS, 0.5, -0.5, -1.0, LN8]
BF = ml_dtypes.bfloat16


def _np_consts():
    e = np.arange(D)[:, None]
    F = np.zeros((D, D), np.float64)
    F[:, 0:513] = np.cos(2 * np.pi * e * np.arange(513)[None, :] / D)
    F[:, 513:1024] = -np.sin(2 * np.pi * e * np.arange(1, 512)[None, :] / D)
    i_ = np.arange(D)[None, :]
    AB = np.zeros((D, D), np.float64)
    w = np.full((513, 1), 2.0); w[0, 0] = 1.0; w[512, 0] = 1.0
    AB[0:513, :] = w * np.cos(2 * np.pi * np.arange(513)[:, None] * i_ / D) / D
    AB[513:1024, :] = -2.0 * np.sin(2 * np.pi * np.arange(1, 512)[:, None] * i_ / D) / D
    LT = (np.arange(128)[:, None] <= np.arange(128)[None, :]).astype(np.float64)
    ST = np.zeros((16, 16 * 128), np.float64)
    for t in range(16):
        ST[:t, t * 128:(t + 1) * 128] = 1.0
    I128 = np.eye(128)
    ONES = np.ones((128, 512))
    SEL2 = np.zeros((34, 128))
    for m in range(128):
        SEL2[m // 64, m] = 1.0
        SEL2[32 + m // 64, m] = 1.0
    BLK2 = np.zeros((128, 2))
    BLK2[0:64, 0] = 1.0; BLK2[64:128, 1] = 1.0
    cast = lambda a: np.ascontiguousarray(a.astype(BF))
    return dict(F=cast(F), AB=cast(AB), LT=cast(LT), ST=cast(ST), I128=cast(I128),
                ONES=cast(ONES), SEL2=cast(SEL2), BLK2=cast(BLK2))


class Sem:
    def __init__(self, h):
        self.h = h
        self.n = 0
    def bump(self, k):
        self.n += k
        return self.n


class Prog:
    def __init__(self):
        self.nc = bass.Bass("TRN2", target_bir_lowering=False, debug=False)
        self.tape = []

    def em(self, eng, fn, waits=(), inc=None, inc_by=1):
        w = tuple(waits)
        post = inc.bump(inc_by) if inc is not None else None
        def run(e, fn=fn, w=w, inc=inc, inc_by=inc_by):
            for s, v in w:
                e.wait_ge(s.h, v)
            ins = fn(e)
            if inc is not None:
                ins.then_inc(inc.h, inc_by)
        self.tape.append((eng, run))
        return post

    def wait(self, eng, waits):
        w = tuple(waits)
        def run(e, w=w):
            for s, v in w:
                e.wait_ge(s.h, v)
        self.tape.append((eng, run))

    def finalize(self, stack):
        block = stack.enter_context(self.nc.Block())
        tape = self.tape
        def make(engname):
            def body(e):
                for eng, fn in tape:
                    if eng == engname:
                        fn(e)
            return body
        block.sync(make("sync"))
        block.gpsimd(make("gpsimd"))
        block.tensor(make("tensor"))
        block.scalar(make("scalar"))
        block.vector(make("vector"))


def build_program(sublayers=("attn_a", "ffn", "attn_b", "ffn"), n_be=NBE, parts=("wq", "wk", "wv", "wo", "wc", "wfq", "wfk", "wfv")):
    import contextlib
    P = Prog()
    nc = P.nc
    stack = contextlib.ExitStack()
    em, wait = P.em, P.wait

    x_in = nc.dram_tensor("x_in", [n_be, S, D], F32, kind="ExternalInput").ap()
    x_out = nc.dram_tensor("x_out", [n_be, S, D], F32, kind="ExternalOutput").ap()
    dbg = nc.dram_tensor("dbg", [128, 16384], BF16, kind="ExternalOutput").ap()
    dbg2 = nc.dram_tensor("dbg2", [128, 1536], F32, kind="ExternalOutput").ap()
    wi = {}
    for nm, shape in [("w_q", [D, D]), ("w_k", [D, D]), ("w_v", [D, D]), ("w_o", [D, D]),
                      ("ln_g", [D]), ("ln_b", [D]), ("q_g", [DH]), ("k_g", [DH]), ("b_o", [D]),
                      ("bWq", [D, D]), ("bWk", [D, D]), ("bWv", [D, D]),
                      ("Wsh", [D, D]), ("Wex", [D, D]), ("fbias", [D])]:
        wi[nm] = nc.dram_tensor(nm, shape, F32, kind="ExternalInput").ap()

    C = _np_consts()
    ch = {k: nc.inline_tensor(v, k + "_c").ap() for k, v in C.items()}
    cn_np = np.zeros((128, 16), np.float32)
    for i, v in enumerate(CONSTS):
        cn_np[:, i] = v
    cn_h = nc.inline_tensor(np.ascontiguousarray(cn_np), "cn_c").ap()

    scr = {}
    for nm in ["wqT", "wkT", "wvT", "woT", "wcT", "wqF", "wkF", "wvF", "wnat"]:
        scr[nm] = nc.dram_tensor(nm + "_s", [D, D], BF16, kind="Internal").ap()
    pc = lambda a: a.rearrange("(c p) d -> p c d", p=128)
    v3 = lambda a, d=1024: a.rearrange("p (c d) -> p c d", d=d)

    A = lambda n, sh, dt: nc.alloc_sbuf_tensor(n, sh, dt).ap()
    xres = A("xres", [128, NT * D], F32)
    xnT = A("xnT", [128, 8 * S], BF16)
    aB = A("aB", [128, 16384], BF16)
    aC = A("aC", [128, 16384], BF16)
    aW = A("aW", [128, 8192], BF16)
    miscb = A("miscb", [128, 3968], BF16)
    misc32 = A("misc32", [128, 1536], F32)
    totals = A("totals", [16, 1024], BF16)
    sel2 = A("sel2", [34, 128], BF16)
    stbuf = A("stbuf", [16, 2048], BF16)
    bvrow = A("bvrow", [1, 1024], BF16)
    ztb = A("ztb", [128, 1024], BF16)
    trash = A("trash", [128, 1024], BF16)

    Pp = lambda n, sh, dt=F32: nc.alloc_psum_tensor(n, sh, dt).ap()
    P1 = Pp("P1", [128, 2048])
    P2 = Pp("P2", [128, 1024])
    P4 = Pp("P4", [128, 1024])
    PTb = P4.bitcast(BF16)

    mI = miscb[:, 0:128]
    mLT = miscb[:, 128:256]
    mBLK2 = miscb[:, 256:258]
    mONE = miscb[:, 384:896]
    mBO = miscb[:, 896:1920]
    mFB = miscb[:, 1920:2944]
    mZT = miscb[:, 2944:3968]

    sSUM = misc32[:, 0:16]
    sSSQ = misc32[:, 16:32]
    sM1 = misc32[:, 32:48]
    sV1 = misc32[:, 48:64]
    sR1 = misc32[:, 64:80]
    sR12 = misc32[:, 80:96]
    sT16 = misc32[:, 96:112]
    sGQ = misc32[:, 112:113]
    sGK = misc32[:, 113:114]
    sGL = misc32[:, 120:128]
    sCN = misc32[:, 152:168]
    sROW = misc32[:, 512:1536]

    sBETAb = A("sbetab", [128, 8], BF16)

    for i, v in enumerate(CONSTS):
        nc.const_aps.register(F32, v, sCN[:, i:i + 1])

    cQT = aC[:, 0:2048]
    cQN = aC[:, 2048:4096]
    cKT = aC[:, 4096:6144]
    cKN = aC[:, 6144:8192]
    cVA = aC[:, 8192:12288]
    cPT = [[aC[:, 0:1024], aC[:, 1024:2048]],
           [aC[:, 4096:5120], aC[:, 5120:6144]]]
    bcrr = [aC[:, 12288 + k * 1024: 13312 + k * 1024] for k in range(4)]
    wSL = [aW[:, k * 2048:(k + 1) * 2048] for k in range(3)]
    wBIG = aW[:, 0:8192]

    d1 = Sem(stack.enter_context(nc.semaphore("d1")))
    d2 = Sem(stack.enter_context(nc.semaphore("d2")))
    st = Sem(stack.enter_context(nc.semaphore("st")))
    ss = Sem(stack.enter_context(nc.semaphore("ss")))
    sv = Sem(stack.enter_context(nc.semaphore("sv")))

    for dst, src in [(mI, ch["I128"]), (mLT, ch["LT"]), (mONE, ch["ONES"]),
                     (mBLK2, ch["BLK2"]), (sel2, ch["SEL2"]),
                     (stbuf, ch["ST"]), (sCN, cn_h)]:
        em("sync", lambda e, dst=dst, src=src: e.dma_start(out=dst, in_=src), inc=d1, inc_by=16)
    em("sync", lambda e: e.dma_start(out=sGL, in_=wi["ln_g"].rearrange("(c p) -> p c", p=128), allow_slow_non_contiguous=True), inc=d1, inc_by=16)
    em("sync", lambda e: e.dma_start(out=sGQ[0:64, :], in_=wi["q_g"].unsqueeze(1), allow_slow_non_contiguous=True), inc=d1, inc_by=16)
    em("sync", lambda e: e.dma_start(out=sGQ[64:128, :], in_=wi["q_g"].unsqueeze(1), allow_slow_non_contiguous=True), inc=d1, inc_by=16)
    em("sync", lambda e: e.dma_start(out=sGK[0:64, :], in_=wi["k_g"].unsqueeze(1), allow_slow_non_contiguous=True), inc=d1, inc_by=16)
    em("sync", lambda e: e.dma_start(out=sGK[64:128, :], in_=wi["k_g"].unsqueeze(1), allow_slow_non_contiguous=True), inc=d1, inc_by=16)
    em("gpsimd", lambda e: e.dma_start(out=mBO[0:1, :], in_=wi["b_o"].unsqueeze(0)), inc=d2, inc_by=16)
    em("gpsimd", lambda e: e.dma_start(out=mFB[0:1, :], in_=wi["fbias"].unsqueeze(0)), inc=d2, inc_by=16)
    c_beta = em("gpsimd", lambda e: e.dma_start(out=sBETAb, in_=wi["ln_b"].rearrange("(c p) -> p c", p=128), allow_slow_non_contiguous=True),
                inc=d2, inc_by=16)
    consts_d1 = d1.n

    wnatb = aB[:, 0:8192]
    wTb = aC[:, 0:8192]
    fpk = wBIG

    def build_w(name_in, out_T=None, gamma=False, out_F=None, add_in=None, bias_row_dst=None, bias_pos=0):
        c1 = em("gpsimd", lambda e: e.dma_start(out=v3(wnatb), in_=pc(wi[name_in])),
                waits=[(d1, d1.n), (st, st.n), (sv, sv.n)], inc=d2, inc_by=16)
        dep = (d2, c1)
        if add_in is not None:
            c2 = em("gpsimd", lambda e: e.dma_start(out=v3(wTb), in_=pc(wi[add_in])), inc=d2, inc_by=16)
            cv = em("vector", lambda e: e.tensor_tensor(wnatb, wnatb, wTb, ALU.add),
                    waits=[(d2, c2)], inc=sv)
            dep = (sv, cv)
        if out_T is not None:
            c_nat = em("sync", lambda e: e.dma_start(out=pc(scr["wnat"]), in_=v3(wnatb)), waits=[dep], inc=d1, inc_by=16)
            tr = None
            for c in range(8):
                tr = em("sync", lambda e, c=c: e.dma_start(out=wTb[:, c * 1024:(c + 1) * 1024],
                                                           in_=scr["wnat"][:, c * 128:(c + 1) * 128],
                                                           transpose=True),
                        waits=[(d1, c_nat)] if c == 0 else [], inc=d1, inc_by=16)
            if bias_row_dst is not None:
                bp = bias_pos
                for half in range(2):
                    for k in range(8):
                        em("tensor", lambda e, half=half, k=k, bp=bp: e.matmul(
                            P2[bp:bp + 1, half * 512:(half + 1) * 512],
                            sBETAb[:, k:k + 1],
                            wTb[:, k * 1024 + half * 512: k * 1024 + (half + 1) * 512],
                            start=(k == 0), stop=(k == 7),
                            tile_position=(0, bp) if bp else None),
                            waits=[(d1, tr), (d2, c_beta), (ss, ss.n)] if (half == 0 and k == 0) else [],
                            inc=st if (half == 1 and k == 7) else None)
                em("scalar", lambda e, bias_row_dst=bias_row_dst, bp=bp: e.activation(
                    out=bias_row_dst, in_=P2[bp:bp + 1, :], func=AF.Copy), waits=[(st, st.n)], inc=ss)
            if gamma:
                for c in range(8):
                    em("vector", lambda e, c=c: e.tensor_scalar_mul(wTb[:, c * 1024:(c + 1) * 1024],
                                                                    wTb[:, c * 1024:(c + 1) * 1024],
                                                                    sGL[:, c:c + 1]),
                       waits=[(st, st.n), (d1, tr), (d1, consts_d1)] if c == 0 else [], inc=sv)
                dep2 = (sv, sv.n)
            else:
                dep2 = (d1, tr)
            em("sync", lambda e, out_T=out_T: e.dma_start(out=pc(scr[out_T]), in_=v3(wTb)),
               waits=[dep2, (sv, sv.n)], inc=d1, inc_by=16)
        if out_F is not None:
            cf = em("sync", lambda e: e.dma_start(out=v3(fpk), in_=pc(ch["F"])), inc=d1, inc_by=16)
            ev = None
            for dsl in range(8):
                for ec in range(8):
                    for half in range(2):
                        if ec == 0 and half == 0:
                            w_ = [(d1, cf), dep] if dsl == 0 else [(sv, ev)]
                        else:
                            w_ = []
                        em("tensor", lambda e, dsl=dsl, ec=ec, half=half: e.matmul(
                            P2[:, half * 512:(half + 1) * 512],
                            wnatb[:, ec * 1024 + dsl * 128: ec * 1024 + dsl * 128 + 128],
                            fpk[:, ec * 1024 + half * 512: ec * 1024 + half * 512 + 512],
                            start=(ec == 0), stop=(ec == 7)),
                            waits=w_,
                            inc=st if (ec == 7 and half == 1) else None)
                cst = st.n
                ev = em("vector", lambda e, dsl=dsl: e.tensor_copy(wTb[:, dsl * 1024:(dsl + 1) * 1024], P2),
                        waits=[(st, cst)], inc=sv)
            em("sync", lambda e, out_F=out_F: e.dma_start(out=pc(scr[out_F]), in_=v3(wTb)),
               waits=[(sv, ev)], inc=d1, inc_by=16)

    if "wq" in parts: build_w("w_q", out_T="wqT", gamma=True, bias_row_dst=mBO[32:33, :], bias_pos=32)
    if "wk" in parts: build_w("w_k", out_T="wkT", gamma=True, bias_row_dst=mBO[64:65, :], bias_pos=64)
    if "wv" in parts: build_w("w_v", out_T="wvT", gamma=True, bias_row_dst=bvrow)
    if "wo" in parts: build_w("w_o", out_T="woT")
    if "wc" in parts: build_w("Wsh", out_T="wcT", add_in="Wex")
    if "wfq" in parts: build_w("bWq", out_F="wqF")
    if "wfk" in parts: build_w("bWk", out_F="wkF")
    if "wfv" in parts: build_w("bWv", out_F="wvF")

    def ln_xnT(double):
        for t in range(NT):
            xt = xres[:, t * D:(t + 1) * D]
            em("scalar", lambda e, xt=xt, t=t: e.activation(out=trash, in_=xt, func=AF.Square,
                                                            accum_out=sSSQ[:, t:t + 1]),
               waits=[(sv, add_sv[t])], inc=ss)
            em("vector", lambda e, xt=xt, t=t: e.reduce_sum(sSUM[:, t:t + 1], xt, axis=AX.X),
               waits=[(sv, add_sv[t])] if False else [], inc=sv)
        cs_stats, cv_stats = ss.n, sv.n
        c_ = em("vector", lambda e: e.tensor_scalar_mul(sM1, sSUM, 1.0 / D), waits=[(sv, cv_stats)], inc=sv)
        c_ = em("vector", lambda e: e.scalar_tensor_tensor(sT16, sM1, 1.0, sM1, ALU.mult, ALU.mult),
                waits=[(sv, c_)], inc=sv)
        c_ = em("vector", lambda e: e.tensor_scalar_mul(sV1, sSSQ, 1.0 / D),
                waits=[(ss, cs_stats), (sv, c_)], inc=sv)
        cv = em("vector", lambda e: e.tensor_tensor(sV1, sV1, sT16, ALU.subtract), waits=[(sv, c_)], inc=sv)
        em("scalar", lambda e: e.activation(out=sT16, in_=sV1, func=AF.Ln, bias=LN_EPS), waits=[(sv, cv)], inc=ss)
        cr1 = em("scalar", lambda e: e.activation(out=sR1, in_=sT16, func=AF.Exp, scale=-0.5), inc=ss)
        if double:
            c_ = em("vector", lambda e: e.scalar_tensor_tensor(sT16, sR1, 1.0, sR1, ALU.mult, ALU.mult),
                    waits=[(ss, cr1)], inc=sv)
            cv = em("vector", lambda e: e.tensor_tensor(sT16, sV1, sT16, ALU.mult), waits=[(sv, c_)], inc=sv)
            em("scalar", lambda e: e.activation(out=sT16, in_=sT16, func=AF.Ln, bias=LN_EPS),
               waits=[(sv, cv)], inc=ss)
            cr2 = em("scalar", lambda e: e.activation(out=sT16, in_=sT16, func=AF.Exp, scale=-0.5), inc=ss)
            cr12 = em("vector", lambda e: e.tensor_tensor(sR12, sR1, sT16, ALU.mult),
                      waits=[(ss, cr2), (sv, cv)], inc=sv)
            rfin, rsem, rcnt = sR12, sv, cr12
        else:
            rfin, rsem, rcnt = sR1, ss, cr1
        for t in range(NT):
            xt = xres[:, t * D:(t + 1) * D]
            cv_ap = em("vector", lambda e, xt=xt, t=t, rfin=rfin: e.tensor_scalar(
                mZT, xt, sM1[:, t:t + 1], rfin[:, t:t + 1], ALU.subtract, ALU.mult),
                waits=([(rsem, rcnt), (st, st.n), (ss, cs_stats)] if t == 0 else [(st, st.n)]), inc=sv)
            ctr = None
            for cch in range(8):
                ctr = em("tensor", lambda e, cch=cch: e.transpose(PTb[:, cch * 128:(cch + 1) * 128],
                                                                  mZT[:, cch * 128:(cch + 1) * 128], mI),
                         waits=[(sv, cv_ap)] if cch == 0 else [], inc=st if cch == 7 else None)
            em("vector", lambda e, t=t: e.tensor_copy(
                xnT.rearrange("p (c s) -> p c s", c=8)[:, :, t * 128:(t + 1) * 128], PTb[:, 0:1024].rearrange("p (c s) -> p c s", c=8)),
               waits=[(st, ctr)], inc=sv)

    add_sv = [0] * NT
    def residual_add(psum_ap, t, waits):
        xt = xres[:, t * D:(t + 1) * D]
        c = em("vector", lambda e, xt=xt, psum_ap=psum_ap: e.tensor_tensor(xt, xt, psum_ap, ALU.add),
               waits=waits, inc=sv)
        add_sv[t] = c
        return c

    def dense_out(wname, bias_row, src_T, wload_waits):
        cwh = em("sync", lambda e, wname=wname: e.dma_start(out=v3(wBIG)[:, 0:4, :], in_=pc(scr[wname])[:, 0:4, :]),
                 waits=wload_waits, inc=d1, inc_by=16)
        cw = em("sync", lambda e, wname=wname: e.dma_start(out=v3(wBIG)[:, 4:8, :], in_=pc(scr[wname])[:, 4:8, :]),
                inc=d1, inc_by=16)
        sv0, ss0 = sv.n, ss.n
        prot = [P2, P1[:, 0:1024], P1[:, 1024:2048]]
        evs = [None, None, None]
        for t in range(NT):
            ps = prot[t % 3]
            w_ = [(d1, cw)] if t == 0 else []
            if t < 3:
                w_.append((sv, sv0)); w_.append((ss, ss0))
            if evs[t % 3] is not None:
                w_.append((sv, evs[t % 3]))
            w_ = [x for x in w_ if x[1] is not None]
            w_ = ([(d1, cwh)] if t == 0 else []) + [x for x in w_ if x[0] is not d1]
            for hlf in range(2):
                em("tensor", lambda e, ps=ps, hlf=hlf, bias_row=bias_row: e.matmul(
                    ps[:, hlf * 512:(hlf + 1) * 512], mONE[0:1, 0:128],
                    bias_row[0:1, hlf * 512:(hlf + 1) * 512], start=True, stop=False),
                    waits=w_ if hlf == 0 else [])
            for cch in range(8):
                for hlf in range(2):
                    em("tensor", lambda e, ps=ps, cch=cch, hlf=hlf, t=t, src_T=src_T: e.matmul(
                        ps[:, hlf * 512:(hlf + 1) * 512],
                        src_T[:, cch * S + t * 128: cch * S + t * 128 + 128],
                        wBIG[:, cch * 1024 + hlf * 512: cch * 1024 + (hlf + 1) * 512],
                        start=False, stop=(cch == 7)),
                        waits=[(d1, cw)] if (t == 0 and cch == 4 and hlf == 0) else [],
                        inc=st if (cch == 7 and hlf == 1) else None)
            evs[t % 3] = residual_add(ps, t, [(st, st.n)])
        return evs

    def ffn():
        ln_xnT(double=False)
        dense_out("wcT", mFB, xnT, [(st, st.n)])

    def attn_a(dbg_stop=False, dbg_stop2=False):
        ln_xnT(double=True)
        em("vector", lambda e: e.memset(cVA.rearrange("p (t c) -> p t c", c=256)[:, :, 64:192], 1.0),
           waits=[(st, st.n), (ss, ss.n)], inc=sv)

        def stream_slices(g, buf, waits=()):
            cnt = None
            for mi, nm in enumerate(["wqT", "wkT", "wvT"]):
                cnt = em("sync", lambda e, nm=nm, g=g, mi=mi, buf=buf: e.dma_start(
                    out=v3(wSL[mi][:, buf * 1024:(buf + 1) * 1024], d=128),
                    in_=pc(scr[nm][:, g * 128:(g + 1) * 128])),
                    waits=waits if mi == 0 else (), inc=d1, inc_by=16)
            return cnt

        slice_cw = [None, None]
        slice_cw[0] = stream_slices(0, 0)
        pair_done_sv = None
        for g in range(NPAIR):
            buf = g % 2
            if g + 1 < NPAIR:
                slice_cw[(g + 1) % 2] = stream_slices(g + 1, (g + 1) % 2, waits=[(st, st.n)])
            sl = lambda mi, buf=buf: wSL[mi][:, buf * 1024:(buf + 1) * 1024]

            pair_sv, pair_ss = sv.n, ss.n

            def proj_mm(mi, bp, w0, sqbuf):
                for q4 in range(4):
                    em("tensor", lambda e, q4=q4, bp=bp, g=g: e.matmul(
                        P1[:, q4 * 512:(q4 + 1) * 512],
                        mBO[bp:bp + 1, g * 128:(g + 1) * 128],
                        mONE[bp:bp + 1, 0:512], start=True, stop=False),
                        waits=w0 if q4 == 0 else [])
                slap = sl(mi)
                for cch in range(8):
                    for q4 in range(4):
                        em("tensor", lambda e, cch=cch, q4=q4, slap=slap: e.matmul(
                            P1[:, q4 * 512:(q4 + 1) * 512],
                            slap[:, cch * 128:(cch + 1) * 128],
                            xnT[:, cch * S + q4 * 512: cch * S + (q4 + 1) * 512],
                            start=False, stop=(cch == 7)),
                            inc=st if (cch == 7 and q4 == 3) else None)
                cm = st.n
                return em("scalar", lambda e, sqbuf=sqbuf: e.activation(out=sqbuf, in_=P1, func=AF.Square),
                          waits=[(st, cm)], inc=ss)

            def chain_mm(c1, gcol, sqbuf, nrm, kside, p2_sv, p2_ss, filler=None):
                cqn = None
                last_cev = None
                for h in range(2):
                    wssq = [(ss, c1), (sv, p2_sv)] + ([(ss, p2_ss)] if last_cev is None else [(ss, last_cev)])
                    em("tensor", lambda e, h=h, sqbuf=sqbuf: e.matmul(
                        P2[0:2, 0:512], mBLK2, sqbuf[:, h * 1024: h * 1024 + 512],
                        start=True, stop=True), waits=wssq)
                    em("tensor", lambda e, h=h, sqbuf=sqbuf: e.matmul(
                        P2[0:2, 512:1024], mBLK2, sqbuf[:, h * 1024 + 512: (h + 1) * 1024],
                        start=True, stop=True), inc=st)
                    cssq = st.n
                    if filler is not None:
                        filler()
                    cvm = em("vector", lambda e: e.tensor_scalar_max(P2[0:2, :], P2[0:2, :], RMS_EPS2),
                             waits=[(st, cssq)], inc=sv)
                    cl = em("scalar", lambda e: e.activation(out=P2[0:2, :], in_=P2[0:2, :], func=AF.Ln),
                            waits=[(sv, cvm)], inc=ss)
                    if kside:
                        cex = em("scalar", lambda e: e.activation(out=sROW[0:2, :], in_=P2[0:2, :],
                                                                  func=AF.Exp, scale=-0.5, bias=LN8),
                                 waits=[(ss, cl)], inc=ss)
                    else:
                        cex = em("scalar", lambda e: e.activation(out=sROW[0:2, :], in_=P2[0:2, :],
                                                                  func=AF.Exp, scale=-0.5),
                                 waits=[(ss, cl)], inc=ss)
                    chi = em("scalar", lambda e: e.activation(out=cPT[0][0][0:2, :], in_=sROW[0:2, :],
                                                              func=AF.Copy), waits=[(ss, cex)], inc=ss)
                    clo = em("vector", lambda e: e.tensor_tensor(cPT[0][1][0:2, :], sROW[0:2, :],
                                                                 cPT[0][0][0:2, :], ALU.subtract),
                             waits=[(ss, chi)], inc=sv)
                    for q2 in range(2):
                        em("tensor", lambda e, q2=q2: e.matmul(
                            P2[:, q2 * 512:(q2 + 1) * 512], sel2[0:2, :],
                            cPT[0][0][0:2, q2 * 512:(q2 + 1) * 512], start=True, stop=False),
                            waits=[(sv, clo)] if q2 == 0 else [])
                        em("tensor", lambda e, q2=q2: e.matmul(
                            P2[:, q2 * 512:(q2 + 1) * 512], sel2[0:2, :],
                            cPT[0][1][0:2, q2 * 512:(q2 + 1) * 512], start=False, stop=True),
                            inc=st if q2 == 1 else None)
                    cbc = st.n
                    cev = em("scalar", lambda e: e.activation(out=sROW, in_=P2, func=AF.Copy),
                             waits=[(st, cbc)], inc=ss)
                    last_cev = cev
                    cqn = em("vector", lambda e, h=h, nrm=nrm, gcol=gcol: e.scalar_tensor_tensor(
                        nrm[:, h * 1024:(h + 1) * 1024], P1[:, h * 1024:(h + 1) * 1024], gcol,
                        sROW, ALU.mult, ALU.mult), waits=[(ss, cev)], inc=sv)
                return cqn

            def vproj_range(t0_, t1_, vstate):
                for t in range(t0_, t1_):
                    slot = t % 8
                    w_ = []
                    if t == 0:
                        w_ = [(sv, pair_sv), (ss, pair_ss)]
                    if t >= 8 and t % 8 == 0:
                        w_ = [(sv, vstate["cvev"])]
                    em("tensor", lambda e, slot=slot, g=g: e.matmul(
                        P4[:, slot * 128:(slot + 1) * 128], mONE[0:1, 0:128],
                        bvrow[0:1, g * 128:(g + 1) * 128], start=True, stop=False), waits=w_)
                    slap2 = sl(2)
                    for cch in range(8):
                        em("tensor", lambda e, t=t, cch=cch, slot=slot, slap2=slap2: e.matmul(
                            P4[:, slot * 128:(slot + 1) * 128],
                            xnT[:, cch * S + t * 128: cch * S + t * 128 + 128],
                            slap2[:, cch * 128:(cch + 1) * 128],
                            start=False, stop=(cch == 7)),
                            inc=st if cch == 7 else None)
                    if t % 8 == 7:
                        cpv = st.n
                        tb = t - 7
                        vA = cVA.rearrange("p (t x) -> p t x", x=256)[:, tb:tb + 8, 0:64]
                        vB = cVA.rearrange("p (t x) -> p t x", x=256)[:, tb:tb + 8, 192:256]
                        pA = P4.rearrange("p (t u c) -> p t u c", t=8, u=2, c=64)[:, :, 0, :]
                        pB = P4.rearrange("p (t u c) -> p t u c", t=8, u=2, c=64)[:, :, 1, :]
                        em("vector", lambda e, vA=vA, pA=pA: e.tensor_copy(vA, pA),
                           waits=[(st, cpv)], inc=sv)
                        vstate["cvev"] = em("vector", lambda e, vB=vB, pB=pB: e.tensor_copy(vB, pB), inc=sv)

            vstate = {"cvev": None, "next": 0}

            def vfill():
                t0_ = vstate["next"]
                if t0_ < NT:
                    vstate["next"] = t0_ + 4
                    vproj_range(t0_, t0_ + 4, vstate)

            w0 = [(d1, slice_cw[buf]), (sv, pair_sv), (ss, pair_ss)]
            c1q = proj_mm(0, 32, w0, cQT)
            cq = chain_mm(c1q, sGQ, cQT, cQN, False, pair_sv, pair_ss, filler=vfill)
            c1k = proj_mm(1, 64, [(sv, cq)], cKT)
            ck = chain_mm(c1k, sGK, cKT, cKN, True, pair_sv, pair_ss, filler=vfill)
            while vstate["next"] < NT:
                vfill()
            crms = ck
            cvev = vstate["cvev"]
            if dbg_stop:
                wait("sync", [(sv, sv.n), (ss, ss.n), (st, st.n)])
                em("sync", lambda e: e.dma_start(out=dbg[:, 0:2048], in_=cQN), inc=d1, inc_by=16)
                em("sync", lambda e: e.dma_start(out=dbg[:, 2048:4096], in_=cKN), inc=d1, inc_by=16)
                em("sync", lambda e: e.dma_start(out=dbg[:, 4096:6176], in_=cVA), inc=d1, inc_by=16)
                return
            PSIM = [P1[:, 0:1024], P1[:, 1024:2048]]
            PPV = [P2, P4]
            simev = [None, None]
            pvdone = [[None, None], [None, None]]
            otv_prev = [cvev, cvev]
            for ih in range(2):
                for jt in range(NT):
                    csims = [[None, None], [None, None]]
                    for hh in range(2):
                        pb = 64 * hh
                        for q2 in range(2):
                            w_ = []
                            if simev[hh] is not None and simev[hh][q2] is not None:
                                w_.append((ss, simev[hh][q2]))
                            elif jt == 0 and hh == 0 and q2 == 0:
                                w_.append((sv, crms))
                            em("tensor", lambda e, hh=hh, q2=q2, pb=pb, jt=jt, ih=ih: e.matmul(
                                PSIM[hh][:, q2 * 512:(q2 + 1) * 512],
                                cKN[pb:pb + 64, jt * 128:(jt + 1) * 128],
                                cQN[pb:pb + 64, ih * 1024 + q2 * 512: ih * 1024 + (q2 + 1) * 512],
                                start=True, stop=True,
                                tile_position=(pb, 0)),
                                waits=w_, inc=st)
                            csims[hh][q2] = st.n
                    for hh in range(2):
                        if simev[hh] is None:
                            simev[hh] = [None, None]
                        for q2 in range(2):
                            wexp = [(st, csims[hh][q2])]
                            if pvdone[hh][jt % 2] is not None:
                                wexp.append((st, pvdone[hh][jt % 2]))
                            simev[hh][q2] = em("scalar", lambda e, hh=hh, jt=jt, q2=q2: e.activation(
                                out=cPT[hh][jt % 2][:, q2 * 512:(q2 + 1) * 512],
                                in_=PSIM[hh][:, q2 * 512:(q2 + 1) * 512], func=AF.Exp),
                                waits=wexp, inc=ss)
                    for hh in range(2):
                        aug = cVA[:, jt * 256 + hh * 128: jt * 256 + (hh + 1) * 128]
                        for q2 in range(2):
                            wpv = [(ss, simev[hh][q2])]
                            if jt == 0 and q2 == 0:
                                wpv.append((sv, otv_prev[hh]))
                            em("tensor", lambda e, hh=hh, q2=q2, jt=jt, aug=aug: e.matmul(
                                PPV[hh][:, q2 * 512:(q2 + 1) * 512], aug,
                                cPT[hh][jt % 2][:, q2 * 512:(q2 + 1) * 512],
                                start=(jt == 0), stop=(jt == NT - 1)),
                                waits=wpv, inc=st if q2 == 1 else None)
                        pvdone[hh][jt % 2] = st.n
                for hh in range(2):
                    i = hh
                    pv15 = pvdone[hh][(NT - 1) % 2]
                    reg = bcrr[2 * hh + ih]
                    if hh == 0:
                        rsrows, otrows = slice(64, 128), slice(0, 64)
                        dst = aB[0:64, g * S + ih * 1024: g * S + (ih + 1) * 1024]
                    else:
                        rsrows, otrows = slice(0, 64), slice(64, 128)
                        dst = aB[64:128, g * S + ih * 1024: g * S + (ih + 1) * 1024]
                    cl = em("scalar", lambda e, i=i, rsrows=rsrows: e.activation(
                        out=PPV[i][rsrows, :], in_=PPV[i][rsrows, :], func=AF.Ln),
                        waits=[(st, pv15)], inc=ss)
                    ce = em("scalar", lambda e, i=i, rsrows=rsrows, reg=reg: e.activation(
                        out=reg[rsrows, :], in_=PPV[i][rsrows, :], func=AF.Exp, scale=-1.0),
                        waits=[(ss, cl)], inc=ss)
                    cdm = em("sync", lambda e, reg=reg, rsrows=rsrows, otrows=otrows: e.dma_start(
                        out=reg[otrows, :], in_=reg[rsrows, :]), waits=[(ss, ce)], inc=d1, inc_by=16)
                    otv_prev[hh] = em("vector", lambda e, i=i, otrows=otrows, reg=reg, dst=dst: e.tensor_tensor(
                        dst, PPV[i][otrows, :], reg[otrows, :], ALU.mult),
                        waits=[(d1, cdm)], inc=sv)
            pair_done_sv = sv.n
            if dbg_stop2 and g == NPAIR - 1:
                wait("sync", [(sv, sv.n), (ss, ss.n), (st, st.n), (d1, d1.n)])
                em("sync", lambda e: e.dma_start(out=dbg, in_=aB), inc=d1, inc_by=16)
                return
        dense_out("woT", mBO, aB, [(st, st.n), (sv, sv.n)])

    def attn_b():
        ln_xnT(double=False)

        slotfree = {0: None, 1: None, 2: None}

        def fft_proj(nm, out_arena, extra0):
            cw = em("sync", lambda e, nm=nm: e.dma_start(out=v3(wBIG), in_=pc(scr[nm])),
                    waits=[(st, st.n)], inc=d1, inc_by=16)
            prot = [P2, P1[:, 0:1024], P1[:, 1024:2048]]
            evs = [None, None, None]
            last = None
            for t in range(NT):
                ps = prot[t % 3]
                w_ = ([(d1, cw)] + extra0) if t == 0 else []
                if t < 3 and slotfree[t % 3] is not None:
                    w_.append(slotfree[t % 3])
                if evs[t % 3] is not None:
                    w_.append((sv, evs[t % 3]))
                for cch in range(8):
                    for hlf in range(2):
                        em("tensor", lambda e, ps=ps, cch=cch, hlf=hlf, t=t: e.matmul(
                            ps[:, hlf * 512:(hlf + 1) * 512],
                            xnT[:, cch * S + t * 128: cch * S + t * 128 + 128],
                            wBIG[:, cch * 1024 + hlf * 512: cch * 1024 + (hlf + 1) * 512],
                            start=(cch == 0), stop=(cch == 7)),
                            waits=w_ if (cch == 0 and hlf == 0) else [],
                            inc=st if (cch == 7 and hlf == 1) else None)
                evs[t % 3] = em("vector", lambda e, ps=ps, t=t, out_arena=out_arena: e.tensor_copy(
                    out_arena[:, t * 1024:(t + 1) * 1024], ps), waits=[(st, st.n)], inc=sv)
                slotfree[t % 3] = (sv, evs[t % 3])
                last = evs[t % 3]
            return last

        cfv = fft_proj("wvF", aC, [])
        cwk = em("sync", lambda e: e.dma_start(out=v3(wBIG), in_=pc(scr["wkF"])), waits=[(st, st.n)], inc=d1, inc_by=16)
        prot = [P2, P1[:, 0:1024], P1[:, 1024:2048]]
        evps = [None, None, None]
        fkv_last = None
        for t in range(NT):
            ps = prot[t % 3]
            w_ = [(d1, cwk)] if t == 0 else []
            if t < 3 and slotfree[t % 3] is not None:
                w_.append(slotfree[t % 3])
            if t == 0:
                w_.append((sv, cfv))
            if evps[t % 3] is not None:
                w_.append((ss, evps[t % 3]))
            for cch in range(8):
                for hlf in range(2):
                    em("tensor", lambda e, ps=ps, cch=cch, hlf=hlf, t=t: e.matmul(
                        ps[:, hlf * 512:(hlf + 1) * 512],
                        xnT[:, cch * S + t * 128: cch * S + t * 128 + 128],
                        wBIG[:, cch * 1024 + hlf * 512: cch * 1024 + (hlf + 1) * 512],
                        start=(cch == 0), stop=(cch == 7)),
                        waits=w_ if (cch == 0 and hlf == 0) else [],
                        inc=st if (cch == 7 and hlf == 1) else None)
            cfk = st.n
            fkt = mZT
            c0 = em("scalar", lambda e, ps=ps: e.activation(out=fkt, in_=ps, func=AF.Copy),
                    waits=[(st, cfk), (sv, sv.n)], inc=ss)
            evps[t % 3] = c0
            slotfree[t % 3] = (ss, c0)
            tile = aC[:, t * 1024:(t + 1) * 1024]
            vre, vim = tile[:, 0:513], tile[:, 513:1024]
            kre, kim = fkt[:, 0:513], fkt[:, 513:1024]
            ss_ = sROW[:, 0:513]
            t2 = sROW[:, 513:1024]
            pbuf = aB[:, 15 * 1024:16 * 1024]
            ppre, ppim = pbuf[:, 0:513], pbuf[:, 513:1024]
            em("vector", lambda e, ss_=ss_, kre=kre: e.tensor_tensor(ss_, kre, kre, ALU.mult),
               waits=[(ss, c0)], inc=sv)
            em("vector", lambda e, t2=t2, kim=kim: e.tensor_tensor(t2, kim, kim, ALU.mult), inc=sv)
            c3 = em("vector", lambda e, ss_=ss_, t2=t2: e.tensor_tensor(ss_[:, 1:512], ss_[:, 1:512], t2, ALU.add), inc=sv)
            em("scalar", lambda e, ss_=ss_: e.activation(out=ss_, in_=ss_, func=AF.Ln), waits=[(sv, c3)], inc=ss)
            em("scalar", lambda e, ss_=ss_: e.activation(out=ss_, in_=ss_, func=AF.Exp, scale=0.5), inc=ss)
            em("scalar", lambda e, ss_=ss_: e.activation(out=ss_, in_=ss_, func=AF.Ln, bias=KF_EPS), inc=ss)
            c7 = em("scalar", lambda e, ss_=ss_: e.activation(out=ss_, in_=ss_, func=AF.Exp, scale=-1.0), inc=ss)
            em("vector", lambda e, ppre=ppre, kre=kre, vre=vre: e.tensor_tensor(ppre, kre, vre, ALU.mult), inc=sv)
            em("vector", lambda e, t2=t2, kim=kim, vim=vim: e.tensor_tensor(t2[:, 0:511], kim, vim, ALU.mult), inc=sv)
            em("vector", lambda e, ppre=ppre, t2=t2: e.tensor_tensor(ppre[:, 1:512], ppre[:, 1:512], t2[:, 0:511], ALU.subtract), inc=sv)
            em("vector", lambda e, ppim=ppim, kre=kre, vim=vim: e.tensor_tensor(ppim, kre[:, 1:512], vim, ALU.mult), inc=sv)
            em("vector", lambda e, t2=t2, kim=kim, vre=vre: e.tensor_tensor(t2[:, 0:511], kim, vre[:, 1:512], ALU.mult), inc=sv)
            em("vector", lambda e, ppim=ppim, t2=t2: e.tensor_tensor(ppim, ppim, t2[:, 0:511], ALU.add), inc=sv)
            em("vector", lambda e, vre=vre, ppre=ppre, ss_=ss_: e.tensor_tensor(vre, ppre, ss_, ALU.mult),
               waits=[(ss, c7)], inc=sv)
            fkv_last = em("vector", lambda e, vim=vim, ppim=ppim, ss_=ss_: e.tensor_tensor(
                vim, ppim, ss_[:, 1:512], ALU.mult), inc=sv)
        cfq = fft_proj("wqF", aB, [(sv, fkv_last)])
        cfqn = None
        for t in range(NT):
            tile = aB[:, t * 1024:(t + 1) * 1024]
            re, im = tile[:, 0:513], tile[:, 513:1024]
            ss_ = sROW[:, 0:513]
            t2 = sROW[:, 513:1024]
            em("vector", lambda e, ss_=ss_, re=re: e.tensor_tensor(ss_, re, re, ALU.mult),
               waits=[(sv, cfq)] if t == 0 else [], inc=sv)
            em("vector", lambda e, t2=t2, im=im: e.tensor_tensor(t2[:, 0:511], im, im, ALU.mult), inc=sv)
            c3 = em("vector", lambda e, ss_=ss_, t2=t2: e.tensor_tensor(ss_[:, 1:512], ss_[:, 1:512], t2[:, 0:511], ALU.add), inc=sv)
            em("scalar", lambda e, ss_=ss_: e.activation(out=ss_, in_=ss_, func=AF.Ln), waits=[(sv, c3)], inc=ss)
            em("scalar", lambda e, ss_=ss_: e.activation(out=ss_, in_=ss_, func=AF.Exp, scale=0.5), inc=ss)
            em("scalar", lambda e, ss_=ss_: e.activation(out=ss_, in_=ss_, func=AF.Ln, bias=KF_EPS), inc=ss)
            c7 = em("scalar", lambda e, ss_=ss_: e.activation(out=ss_, in_=ss_, func=AF.Exp, scale=-1.0), inc=ss)
            em("vector", lambda e, re=re, ss_=ss_: e.tensor_tensor(re, re, ss_, ALU.mult), waits=[(ss, c7)], inc=sv)
            cfqn = em("vector", lambda e, im=im, ss_=ss_: e.tensor_tensor(im, im, ss_[:, 1:512], ALU.mult), inc=sv)
        ctot = None
        ev_prev = None
        for t in range(NT):
            w_ = [(sv, cfqn)] if t == 0 else []
            if t == 0 and slotfree[0] is not None:
                w_.append(slotfree[0])
            if ev_prev is not None:
                w_.append((ss, ev_prev))
            for hlf in range(2):
                em("tensor", lambda e, t=t, hlf=hlf: e.matmul(
                    P2[:, hlf * 512:(hlf + 1) * 512], mLT,
                    aC[:, t * 1024 + hlf * 512: t * 1024 + (hlf + 1) * 512],
                    start=True, stop=True),
                    waits=w_ if hlf == 0 else [], inc=st if hlf == 1 else None)
            ccum = st.n
            ev_prev = em("scalar", lambda e, t=t: e.activation(out=aC[:, t * 1024:(t + 1) * 1024], in_=P2,
                                                               func=AF.Copy), waits=[(st, ccum)], inc=ss)
            ctot = em("sync", lambda e, t=t: e.dma_start(out=totals[t:t + 1, :],
                                                         in_=aC[127:128, t * 1024:(t + 1) * 1024]),
                      waits=[(ss, ev_prev)], inc=d1, inc_by=16)
        cab = em("sync", lambda e: e.dma_start(out=v3(wBIG), in_=pc(ch["AB"])), waits=[(st, st.n)], inc=d1, inc_by=16)
        prot2 = [P2, P1[:, 0:1024]]
        evs2 = [None, None]
        ctr_prev = None
        ztc_prev = None
        for t in range(NT):
            tile = aC[:, t * 1024:(t + 1) * 1024]
            ps = prot2[t % 2]
            w_ = [(d1, ctot)] if t == 0 else []
            if evs2[t % 2] is not None:
                w_.append((sv, evs2[t % 2]))
            if t > 0:
                for hlf in range(2):
                    em("tensor", lambda e, t=t, hlf=hlf, ps=ps: e.matmul(
                        ps[:, hlf * 512:(hlf + 1) * 512], stbuf[:, t * 128:(t + 1) * 128],
                        totals[:, hlf * 512:(hlf + 1) * 512], start=True, stop=True),
                        waits=w_ if hlf == 0 else [], inc=st if hlf == 1 else None)
                coff = st.n
                czadd = em("vector", lambda e, tile=tile, ps=ps: e.tensor_tensor(tile, tile, ps, ALU.add),
                           waits=[(st, coff)], inc=sv)
            else:
                wait("vector", w_)
                czadd = sv.n
            zbuf = mZT
            fq = aB[:, t * 1024:(t + 1) * 1024]
            cre, cim = tile[:, 0:513], tile[:, 513:1024]
            qre, qim = fq[:, 0:513], fq[:, 513:1024]
            zre, zim = zbuf[:, 0:513], zbuf[:, 513:1024]
            t2 = sROW[:, 0:511]
            em("vector", lambda e, zre=zre, cre=cre, qre=qre: e.tensor_tensor(zre, cre, qre, ALU.mult),
               waits=[(sv, czadd)] + ([(st, ctr_prev)] if ctr_prev is not None else []), inc=sv)
            em("vector", lambda e, t2=t2, cim=cim, qim=qim: e.tensor_tensor(t2, cim, qim, ALU.mult), inc=sv)
            em("vector", lambda e, zre=zre, t2=t2: e.tensor_tensor(zre[:, 1:512], zre[:, 1:512], t2, ALU.add), inc=sv)
            em("vector", lambda e, zim=zim, cim=cim, qre=qre: e.tensor_tensor(zim, cim, qre[:, 1:512], ALU.mult), inc=sv)
            em("vector", lambda e, t2=t2, cre=cre, qim=qim: e.tensor_tensor(t2, cre[:, 1:512], qim, ALU.mult), inc=sv)
            zl = em("vector", lambda e, zim=zim, t2=t2: e.tensor_tensor(zim, zim, t2, ALU.subtract), inc=sv)
            ctr = None
            for cch in range(8):
                ctr = em("tensor", lambda e, cch=cch, zbuf=zbuf: e.transpose(
                    PTb[:, cch * 128:(cch + 1) * 128], zbuf[:, cch * 128:(cch + 1) * 128], mI),
                    waits=([(sv, zl)] + ([(sv, ztc_prev)] if ztc_prev is not None else [])) if cch == 0 else [],
                    inc=st if cch == 7 else None)
            ctr_prev = ctr
            ztc = em("vector", lambda e: e.tensor_copy(ztb, PTb[:, 0:1024]), waits=[(st, ctr)], inc=sv)
            ztc_prev = ztc
            w2 = [(sv, ztc), (d1, cab)]
            for hlf in range(2):
                for cch in range(8):
                    em("tensor", lambda e, ps=ps, cch=cch, hlf=hlf: e.matmul(
                        ps[:, hlf * 512:(hlf + 1) * 512],
                        ztb[:, cch * 128:(cch + 1) * 128],
                        wBIG[:, cch * 1024 + hlf * 512: cch * 1024 + (hlf + 1) * 512],
                        start=(cch == 0), stop=False),
                        waits=w2 if (cch == 0 and hlf == 0) else [])
                for cc2 in range(4):
                    cch = hlf * 4 + cc2
                    em("tensor", lambda e, ps=ps, cch=cch, t=t: e.matmul(
                        ps[:, cch * 128:(cch + 1) * 128],
                        xnT[:, cch * S + t * 128: cch * S + t * 128 + 128],
                        mI, start=False, stop=(cch % 4 == 3)),
                        inc=st if cch == 7 else None)
            evs2[t % 2] = residual_add(ps, t, [(st, st.n)])

    def lncheck():
        ln_xnT(double=False)
        wait("sync", [(sv, sv.n)])
        em("sync", lambda e: e.dma_start(out=dbg, in_=xnT), inc=d1, inc_by=16)
        em("sync", lambda e: e.dma_start(out=dbg2, in_=misc32), inc=d1, inc_by=16)

    def lncheck2():
        ln_xnT(double=True)
        wait("sync", [(sv, sv.n)])
        em("sync", lambda e: e.dma_start(out=dbg, in_=xnT), inc=d1, inc_by=16)

    sub_map = {"attn_a": attn_a, "ffn": ffn, "attn_b": attn_b, "lncheck": lncheck, "lncheck2": lncheck2,
               "qk0": lambda: attn_a(dbg_stop=True), "attn0": lambda: attn_a(dbg_stop2=True)}
    for be in range(n_be):
        wait("sync", [(sv, sv.n), (ss, ss.n), (st, st.n)])
        cx = None
        for t in range(NT):
            cx = em("sync", lambda e, be=be, t=t: e.dma_start(
                out=xres[:, t * D:(t + 1) * D], in_=x_in[be, t * 128:(t + 1) * 128, :]),
                inc=d1, inc_by=16)
        for eng in ("tensor", "scalar", "vector", "gpsimd"):
            wait(eng, [(d1, d1.n), (d2, d2.n)])
        for name in sublayers:
            sub_map[name]()
        wait("sync", [(sv, sv.n)])
        cso = None
        for t in range(NT):
            cso = em("sync", lambda e, be=be, t=t: e.dma_start(
                out=x_out[be, t * 128:(t + 1) * 128, :], in_=xres[:, t * D:(t + 1) * D]),
                inc=d1, inc_by=16)
        wait("sync", [(d1, cso)])
    P.finalize(stack)
    stack.close()
    return nc


def kernel(**inputs):
    x = np.ascontiguousarray(inputs["L0_attn_a_x"], dtype=np.float32)
    n_cores = 8
    shards = np.split(x, n_cores, axis=0)
    nc = build_program()
    base = {
        "w_q": inputs["L0_attn_a_w_q"], "w_k": inputs["L0_attn_a_w_k"],
        "w_v": inputs["L0_attn_a_w_v"], "w_o": inputs["L0_attn_a_w_o"],
        "ln_g": inputs["L0_attn_a_ln_gamma"], "ln_b": inputs["L0_attn_a_ln_beta"],
        "q_g": inputs["L0_attn_a_q_gamma"], "k_g": inputs["L0_attn_a_k_gamma"],
        "b_o": inputs["L0_attn_a_b_o"],
        "bWq": inputs["L0_attn_b_Wq"], "bWk": inputs["L0_attn_b_Wk"], "bWv": inputs["L0_attn_b_Wv"],
        "Wsh": inputs["L0_ffn_W_shared"], "Wex": inputs["L0_ffn_W_expert"],
        "fbias": inputs["L0_ffn_bias"],
    }
    base = {k: np.ascontiguousarray(v, dtype=np.float32) for k, v in base.items()}
    in_maps = [dict(base, x_in=np.ascontiguousarray(s)) for s in shards]
    res = run_bass_kernel_spmd(nc, in_maps, core_ids=list(range(n_cores)))
    out = np.concatenate([np.asarray(r["x_out"]) for r in res.results], axis=0)
    return out.astype(np.float32)
